# version 06
# baseline (speedup 1.0000x reference)
import numpy as np
import ml_dtypes
import concourse.bass as bass
import concourse.mybir as mybir
from concourse.bass_utils import run_bass_kernel_spmd

F32 = mybir.dt.float32
BF16 = mybir.dt.bfloat16
I32 = mybir.dt.int32
AF = mybir.ActivationFunctionType
OP = mybir.AluOpType
AX = mybir.AxisListType

B, L, D, N, R = 4, 4096, 2048, 16, 64
DH = D // 2
TC = 512
NCH = L // TC
NDC = DH // 128
NDCF = D // 128
NH = N // 2
E, H, HS = 8, 1408, 2816
NHS = H // 128
WDS = 6
NTOK = L // 2
NGRP = 2
GTOK = NTOK // NGRP
NTIL = GTOK // 128
EPS = 1e-5
NSLOT = 10
ABASE = 17472
PARANOID = 0
FFN_GATE = True


def build_program(mode):
    assert mode in ("mamba", "ffn", "fused")
    do_m = mode in ("mamba", "fused")
    do_f = mode in ("ffn", "fused")
    fused = mode == "fused"

    nc = bass.Bass("TRN2", target_bir_lowering=False, debug=False)
    ap = {}

    def din(name, shape, dt=F32):
        ap[name] = nc.dram_tensor(name, shape, dt, kind="ExternalInput").ap()

    def dout(name, shape, dt=F32):
        ap[name] = nc.dram_tensor(name, shape, dt, kind="ExternalOutput").ap()

    din("ident", [128, 128]); din("identb", [128, 128], BF16)
    din("ones1", [1, 128]); din("ones1b", [1, 128], BF16)
    din("cinfo", [1, 8], I32)
    if do_m:
        din("x", [L, D]); din("xh", [L, DH])
        din("wxT", [D, 96]); din("wdT", [R, DH])
        din("bdt", [128, NDC]); din("alog", [128, NDC * N]); din("dprm", [1, DH])
        din("seln", [16, N * 128], BF16)
    if do_f:
        din("gwT", [D, E]); din("sgw", [128, NDCF])
        din("sel8", [8, E * 128], BF16)
        din("wg", [E, D, H]); din("wu", [E, D, H]); din("wd", [E, H, D])
        din("swg", [D, HS]); din("swu", [D, HS]); din("swd", [HS, D])
        dout("oy", [NTOK, D])
    if mode == "mamba":
        dout("zout", [L, DH])
    elif mode == "ffn":
        din("zin", [NTOK, D])
    else:
        ap["zsh"] = nc.dram_tensor("zsh", [L, D], F32, addr_space="Shared").ap()
        ap["flg"] = nc.dram_tensor("flg", [2, 16], I32, addr_space="Shared").ap()

    pos = [ABASE]
    sb_off = {}

    def sb(name, shape, dtype, at=None):
        free = int(np.prod(shape[1:]))
        nbytes = free * (2 if dtype == BF16 else 4)
        if at is None:
            at = pos[0]
            pos[0] += (nbytes + 63) & ~63
        else:
            pos[0] = max(pos[0], at + ((nbytes + 63) & ~63))
        sb_off[name] = at
        return nc.alloc_sbuf_tensor_at(name, [shape[0]] + list(shape[1:]),
                                       dtype, offset=at)

    ident_s = sb("ident_s", [128, 128], F32)
    identb_s = sb("identb_s", [128, 128], BF16)
    ones1_s = sb("ones1_s", [1, 128], F32)
    ones1b_s = sb("ones1b_s", [1, 128], BF16)
    cinfo_s = sb("cinfo_s", [1, 8], I32)
    pollt = sb("pollt", [1, 4], I32)
    if do_f:
        gwT_s = sb("gwT_s", [128, NDCF, E], F32)
        sgw_s = sb("sgw_s", [128, NDCF], F32)
        sel8_s = sb("sel8_s", [8, E * 128], BF16)
    common_end = pos[0]

    if do_m:
        pos[0] = common_end
        wxT_s = sb("wxT_s", [128, NDCF, 96], BF16)
        wdT_s = sb("wdT_s", [64, DH], BF16)
        bdt_s = sb("bdt_s", [128, NDC], F32)
        acol_s = sb("acol_s", [128, NDC * N], F32)
        opd_row = sb("opd_row", [1, DH], F32)
        opd_rep = sb("opd_rep", [128, DH], F32)
        seln_s = sb("seln_s", [16, N * 128], BF16)
        draw_s = sb("draw_s", [64, L], BF16)
        bm_s = sb("bm_s", [16, L], BF16)
        cm_s = sb("cm_s", [16, L], BF16)
        state_s = sb("state_s", [128, NDC, N], F32)
        x_tm = sb("x_tm", [128, 4, DH], F32)
        x_tmb = sb("x_tmb", [128, 4, DH], BF16)
        xfmh = sb("xfmh", [128, NDC, TC], BF16)
        delta_s = sb("delta_s", [128, NDC, TC], F32)
        u_all = sb("u_all", [128, NDC, TC], BF16)
        dA = sb("dA", [128, NH, TC], F32)
        dbx = sb("dbx", [128, 2, NH * TC], BF16)
        hsc = sb("hsc", [128, NH, TC], BF16)
        yac = sb("yac", [128, NDC, TC], F32)
        wtm = sb("wtm", [128, 4, DH], F32)
        brep = sb("brep", [128, NH, TC], BF16)
        crep = sb("crep", [128, NH, TC], BF16)
        ytmp = sb("ytmp", [128, 4, TC], BF16)
        sp1 = sb("sp1", [128, 2, TC], F32)
        sp2 = sb("sp2", [128, 2, TC], F32)
        m0_xb = sb("m0_xb", [128, 4, D], BF16, at=sb_off["dA"])
        m0_xf = sb("m0_xf", [128, NDCF, TC], BF16, at=sb_off["yac"])
        m_end = pos[0]
    else:
        m_end = common_end

    if do_f:
        pos[0] = common_end
        acc = sb("acc", [128, NDCF, GTOK], F32)
        x2b = sb("x2b", [128, NDCF, GTOK], BF16)
        hpb = sb("hpb", [128, NHS, GTOK], BF16)
        wde = sb("wde", [128, 6, D], BF16)
        wgb = sb("wgb", [128, 1, NDCF * 128], BF16)
        wub = sb("wub", [128, 1, NDCF * 128], BF16)
        comb_e = sb("comb_e", [128, 2, GTOK], BF16)
        sg_rep = sb("sg_rep", [128, GTOK], BF16)
        ztile = sb("ztile", [128, 2, D], F32)
        x2t = sb("x2t", [128, 1, D], F32)
        outt = sb("outt", [128, D], F32)
        hpt = sb("hpt", [128, GTOK], BF16)
        stat = sb("stat", [128, 16], F32)
        lg_fm = sb("lg_fm", [8, GTOK], F32)
        lgT = sb("lgT", [128, NTIL, E], F32)
        carg = sb("carg", [128, NTIL, E], F32)
        cmask = sb("cmask", [128, NTIL, E], F32)
        comb_fm = sb("comb_fm", [8, GTOK], BF16)
        sg_row = sb("sg_row", [1, GTOK], BF16)
        f_end = pos[0]
    else:
        f_end = common_end

    assert max(m_end, f_end) <= 229376, (m_end, f_end)

    P0 = nc.alloc_psum_tensor("P0", [128, 1024], F32)
    P1 = nc.alloc_psum_tensor("P1", [128, 1024], F32)
    P2 = nc.alloc_psum_tensor("P2", [128, 1024], F32)
    P3 = nc.alloc_psum_tensor("P3", [128, 1024], F32)
    PB = [P0, P1, P2, P3]

    S = {}
    for nm in ["LD", "SET",
               "M0X", "M0T", "M0E", "M0M", "M0V",
               "XTF", "XTB", "UTR", "UEV", "DMM", "SPS", "SPA", "SPP",
               "BCM", "BCE", "TSA", "DAS", "SCN", "YDC", "ZTR", "WTM", "ZST",
               "ZDM", "FLG", "PGO", "PRT",
               "ZLD", "LNS", "LNN", "X2T", "X2E", "X2F",
               "GLM", "GLE", "GTT", "TOP", "SIG", "CMB",
               "CBC", "CBE", "SGM", "SGE",
               "WLG", "WLU", "WLD", "GUM", "UUM", "SIL", "HT1", "HPD",
               "DNM", "DAC", "OTR", "OEV", "LN2S", "LN2N", "ODM",
               "MIX", "MICS", "CFME", "LDG", "PAR", "DBX"]:
        S[nm] = nc.alloc_semaphore(nm)

    setup = [("ident", ident_s[:], None), ("identb", identb_s[:], None),
             ("ones1", ones1_s[:], None), ("ones1b", ones1b_s[:], None),
             ("cinfo", cinfo_s[:], None)]
    if do_m:
        setup += [("bdt", bdt_s[:], None),
                  ("alog", acol_s[:], None), ("dprm", opd_row[:], None),
                  ("seln", seln_s[:], None)]
    if do_f:
        setup += [
            ("gwT", gwT_s[:], ap["gwT"].rearrange("(c p) e -> p c e", p=128)),
            ("sgw", sgw_s[:], None),
            ("sel8", sel8_s[:], None),
        ]
    NLD = len(setup) * 16

    def kd(k, dc):
        return k * NDC + dc

    def kt(k, ts_):
        return k * 4 + ts_

    sqrt_wait = []

    with nc.Block() as block:

        @block.sync
        def _(sync):
            for name, dst, src in setup:
                sync.dma_start(out=dst, in_=(src if src is not None else ap[name])
                               ).then_inc(S["LD"], 16)
            if fused:
                rpar = sync.alloc_register("rpar")
                sync.wait_ge(S["LD"], 16 * 5)
                sync.load(rpar, cinfo_s[0:1, 0:1])
                dhv = sync.snap(rpar, min_val=0, max_val=1)
                zsh_st = ap["zsh"].rearrange(
                    "(c s p) (h d) -> h c p s d", p=128, s=4, h=2)[dhv]

            def zstore(k):
                src = x_tm[:]
                if mode == "mamba":
                    dst = ap["zout"][k * TC:(k + 1) * TC, :] \
                        .rearrange("(s p) d -> p s d", p=128)
                else:
                    dst = zsh_st[k]
                sync.dma_start(out=dst, in_=src).then_inc(S["ZDM"], 16)

            if do_m:
                for k in range(NCH):
                    if k >= 1:
                        sync.wait_ge(S["ZST"], 4 * k)
                        zstore(k - 1)
                        sync.wait_ge(S["ZDM"], 16 * k)
                    sync.dma_start(
                        out=x_tm[:],
                        in_=ap["xh"][k * TC:(k + 1) * TC, :]
                        .rearrange("(s p) d -> p s d", p=128),
                    ).then_inc(S["XTF"], 16)
                sync.wait_ge(S["ZST"], 4 * NCH)
                zstore(NCH - 1)
            if fused:
                sync.wait_ge(S["ZDM"], 16 * NCH)
                sync.dma_start(out=ap["flg"][dhv, 0:1],
                               in_=cinfo_s[0:1, 2:3]).then_inc(S["FLG"], 16)
            if do_f:
                for g in range(NGRP):
                    for i in range(NTIL):
                        gi = g * NTIL + i
                        sync.wait_ge(S["LN2N"], gi + 1)
                        sync.dma_start(out=ap["oy"][gi * 128:(gi + 1) * 128, :],
                                       in_=ztile[:, gi % 2, :]).then_inc(S["ODM"], 16)
                sync.wait_ge(S["ODM"], 16 * NGRP * NTIL)
            elif mode == "mamba":
                sync.wait_ge(S["ZDM"], 16 * NCH)

        @block.gpsimd
        def _(gpsimd):
            if do_m:
                gpsimd.dma_start(
                    out=wxT_s[:],
                    in_=ap["wxT"].rearrange("(c p) e -> p c e", p=128),
                ).then_inc(S["LDG"], 16)
                gpsimd.dma_start(out=wdT_s[:], in_=ap["wdT"]).then_inc(S["LDG"], 16)
                for k in range(NCH):
                    if k >= 1:
                        gpsimd.wait_ge(S["M0T"], 4 * k)
                    gpsimd.dma_start(
                        out=m0_xb[:],
                        in_=ap["x"][k * TC:(k + 1) * TC, :]
                        .rearrange("(s p) d -> p s d", p=128),
                    ).then_inc(S["M0X"], 16)
                for k in range(NCH):
                    if k >= 1:
                        gpsimd.wait_ge(S["UTR"], 4 * k)
                    gpsimd.dma_start(
                        out=x_tmb[:],
                        in_=ap["xh"][k * TC:(k + 1) * TC, :]
                        .rearrange("(s p) d -> p s d", p=128),
                    ).then_inc(S["XTB"], 16)
                    def dbx_dc(h, dc):
                        kh = 2 * k + h
                        ih = kh * NDC + dc
                        gpsimd.wait_ge(S["BCE"], 8 * kh + 4)
                        if ih >= 2:
                            gpsimd.wait_ge(S["SCN"], ih - 1)
                        for n in range(NH):
                            ins = gpsimd.tensor_mul(
                                out=dbx[:, dc % 2, n * TC:(n + 1) * TC],
                                in0=u_all[:, dc, :], in1=brep[:, n, :])
                        ins.then_inc(S["DBX"], 1)

                    for dc in range(NDC):
                        i = kd(k, dc)
                        gpsimd.wait_ge(S["SPA"], i + 1)
                        gpsimd.tensor_add(out=delta_s[:, dc, :],
                                          in0=sp1[:, i % 2, :],
                                          in1=sp2[:, i % 2, :]).then_inc(S["SPP"], 1)
                        gpsimd.wait_ge(S["UEV"], 4 * (k + 1))
                        gpsimd.tensor_mul(out=u_all[:, dc, :],
                                          in0=delta_s[:, dc, :],
                                          in1=xfmh[:, dc, :])
                        dbx_dc(0, dc)
                    for dc in range(NDC):
                        dbx_dc(1, dc)
                    gpsimd.wait_ge(S["SET"], 5)
                    gpsimd.wait_ge(S["XTF"], 16 * (k + 1))
                    for ts_ in range(4):
                        if k >= 1:
                            gpsimd.wait_ge(S["ZST"], kt(k - 1, ts_) + 1)
                        gpsimd.tensor_mul(out=wtm[:, ts_, :], in0=x_tm[:, ts_, :],
                                          in1=opd_rep[:]).then_inc(S["WTM"], 1)
            if do_f:
                if fused:
                    gpsimd.wait_ge(S["FLG"], 16)
                    rpn = gpsimd.alloc_register("rpn")
                    rnon = gpsimd.alloc_register("rnon")
                    rcnt = gpsimd.alloc_register("rcnt")
                    rval = gpsimd.alloc_register("rval")
                    gpsimd.load(rpn, cinfo_s[0:1, 1:2])
                    gpsimd.load(rnon, cinfo_s[0:1, 2:3])
                    pnv = gpsimd.snap(rpn, min_val=0, max_val=1)
                    gpsimd.reg_mov(rcnt, 0)
                    gpsimd.reg_mov(rval, 0)

                    def cond():
                        a = gpsimd.snap(rval, min_val=-2**31, max_val=2**31 - 1)
                        bq = gpsimd.snap(rnon, min_val=-2**31, max_val=2**31 - 1)
                        return a != bq
                    with gpsimd.While(cond):
                        gpsimd.reg_add(rcnt, rcnt, 16)
                        gpsimd.dma_start(out=pollt[0:1, 0:1],
                                         in_=ap["flg"][pnv, 0:1]
                                         ).then_inc(S["PRT"], 16)
                        gpsimd.wait_ge(S["PRT"], rcnt)
                        gpsimd.load(rval, pollt[0:1, 0:1])
                    gpsimd.sem_inc(S["PGO"], 1)
                for g in range(NGRP):
                    for sl in range(NSLOT):
                        shared = sl >= E
                        je = g * NSLOT + sl

                        def wd_dma(hs):
                            hh = hs if not shared else (sl - E) * NHS + hs
                            slot = hs if hs <= 5 else hs - 6
                            if shared:
                                srcd = ap["swd"][hh * 128:(hh + 1) * 128, :]
                            else:
                                srcd = ap["wd"][sl, hh * 128:(hh + 1) * 128, :]
                            if hs <= 5:
                                if je >= 1:
                                    gpsimd.wait_ge(S["DNM"], je * 2 * NDCF)
                            else:
                                gpsimd.wait_ge(S["DNM"], (je * 2 + 1) * NDCF)
                            gpsimd.dma_start(out=wde[:, slot, :], in_=srcd
                                             ).then_inc(S["WLD"], 16)

                        for hs in range(NHS):
                            hh = hs if not shared else (sl - E) * NHS + hs
                            j = je * NHS + hs
                            if j >= 1:
                                gpsimd.wait_ge(S["GUM"], j)
                            if shared:
                                srcg = ap["swg"][:, hh * 128:(hh + 1) * 128]
                                srcu = ap["swu"][:, hh * 128:(hh + 1) * 128]
                            else:
                                srcg = ap["wg"][sl, :, hh * 128:(hh + 1) * 128]
                                srcu = ap["wu"][sl, :, hh * 128:(hh + 1) * 128]
                            gpsimd.dma_start(
                                out=wgb[:, 0, :]
                                .rearrange("p (c h) -> p c h", c=NDCF),
                                in_=srcg.rearrange("(c p) h -> p c h", p=128),
                            ).then_inc(S["WLG"], 16)
                            if j >= 1:
                                gpsimd.wait_ge(S["UUM"], j)
                            gpsimd.dma_start(
                                out=wub[:, 0, :]
                                .rearrange("p (c h) -> p c h", c=NDCF),
                                in_=srcu.rearrange("(c p) h -> p c h", p=128),
                            ).then_inc(S["WLU"], 16)
                            if hs <= 5:
                                wd_dma(hs)
                        for hs in range(6, NHS):
                            wd_dma(hs)

        @block.tensor
        def _(tensor):
            if do_m:
                tensor.wait_ge(S["SET"], 3)
                tensor.matmul(P1[:, 0:512], ones1_s[:], opd_row[0:1, 0:512],
                              start=True, stop=True)
                tensor.matmul(P1[:, 512:1024], ones1_s[:], opd_row[0:1, 512:1024],
                              start=True, stop=True).then_inc(S["SET"], 1)
                tensor.wait_ge(S["LD"], NLD)
                tensor.wait_ge(S["LDG"], 32)
                for k in range(NCH):
                    for ts_ in range(4):
                        it = 4 * k + ts_
                        tensor.wait_ge(S["M0X"], 16 * (k + 1))
                        if it >= 2:
                            tensor.wait_ge(S["M0E"], it - 1)
                        sv = PB[3 - (ts_ % 2)][:].bitcast(BF16)
                        for dc in range(NDCF):
                            ins = tensor.transpose(
                                sv[:, dc * 128:(dc + 1) * 128],
                                m0_xb[:, ts_, dc * 128:(dc + 1) * 128],
                                identb_s[:])
                        ins.then_inc(S["M0T"], 1)
                    tensor.wait_ge(S["M0E"], 4 * (k + 1))
                    if k == 0:
                        tensor.wait_ge(S["SET"], 5)
                    else:
                        tensor.wait_ge(S["M0V"], k)
                    for c in range(NDCF):
                        tensor.matmul(P0[0:80, 0:512], wxT_s[:, c, 0:80],
                                      m0_xf[:, c, :], start=(c == 0),
                                      stop=(c == NDCF - 1))
                    for c in range(NDCF):
                        ins = tensor.matmul(P1[0:16, 0:512], wxT_s[:, c, 80:96],
                                            m0_xf[:, c, :], start=(c == 0),
                                            stop=(c == NDCF - 1))
                    ins.then_inc(S["M0M"], 1)
                for k in range(NCH):
                    for ts_ in range(4):
                        it = kt(k, ts_)
                        tensor.wait_ge(S["XTB"], 16 * (k + 1))
                        if it >= 2:
                            tensor.wait_ge(S["UEV"], it - 1)
                        elif k == 0:
                            tensor.wait_ge(S["M0E"], 4 * NCH)
                        half = (ts_ % 2) * 1024
                        sv = P3[:].bitcast(BF16)
                        for dc in range(NDC):
                            ins = tensor.transpose(
                                sv[:, half + dc * 128:half + (dc + 1) * 128],
                                x_tmb[:, ts_, dc * 128:(dc + 1) * 128],
                                identb_s[:])
                        ins.then_inc(S["UTR"], 1)
                    def bc_build(h, pieces=range(8)):
                        kh = k * 2 + h
                        for p in pieces:
                            if k == 0 and h == 0 and p == 0:
                                tensor.wait_ge(S["M0V"], NCH)
                            if 8 * kh + p >= 2:
                                tensor.wait_ge(S["BCE"], 8 * kh + p - 1)
                            if k >= 1 and h == 0 and p <= 1:
                                tensor.wait_ge(S["ZST"], 4 * k)
                            src = bm_s if p < 4 else cm_s
                            pp = p % 4
                            tgt = P1 if p % 2 == 0 else P2
                            for h2 in range(2):
                                ng = h * NH + 2 * pp + h2
                                ins = tensor.matmul(
                                    tgt[:, h2 * 512:(h2 + 1) * 512],
                                    seln_s[:, ng * 128:(ng + 1) * 128],
                                    src[:, k * TC:(k + 1) * TC],
                                    start=True, stop=True)
                            ins.then_inc(S["BCM"], 1)

                    def delta_mm(dc):
                        i = kd(k, dc)
                        if i == 0:
                            tensor.wait_ge(S["M0V"], NCH)
                        if i >= 2:
                            tensor.wait_ge(S["SPS"], i - 1)
                        tensor.matmul(
                            P0[:, (dc % 2) * 512:(dc % 2 + 1) * 512],
                            wdT_s[:, dc * 128:(dc + 1) * 128],
                            draw_s[:, k * TC:(k + 1) * TC],
                            start=True, stop=True).then_inc(S["DMM"], 1)

                    bc_build(0, range(0, 4))
                    delta_mm(0)
                    delta_mm(1)
                    bc_build(0, range(4, 8))
                    for dc in range(2, NDC):
                        delta_mm(dc)
                    bc_build(1)
                    tensor.wait_ge(S["YDC"], 2 * NDC * (k + 1))
                    tensor.wait_ge(S["BCE"], 8 * (2 * k + 2))
                    for ts_ in range(4):
                        it = kt(k, ts_)
                        if it >= 2:
                            tensor.wait_ge(S["ZST"], it - 1)
                        tgt = P1 if ts_ % 2 == 0 else P2
                        for dc in range(NDC):
                            ins = tensor.transpose(
                                tgt[:, dc * 128:(dc + 1) * 128],
                                yac[:, dc, ts_ * 128:(ts_ + 1) * 128],
                                ident_s[:])
                        ins.then_inc(S["ZTR"], 1)
            if do_f:
                if fused:
                    tensor.wait_ge(S["PGO"], 1)
                tensor.wait_ge(S["LD"], NLD)
                for g in range(NGRP):
                    gof = g * NTIL
                    if g >= 1:
                        tensor.wait_ge(S["DAC"], g * NSLOT * 2 * NDCF)
                        tensor.wait_ge(S["OEV"], 16 * g)
                    for i in range(NTIL):
                        gi = gof + i
                        tensor.wait_ge(S["LNN"], gi + 1)
                        pa, pb_ = (P0, P1) if i % 2 == 0 else (P2, P3)
                        if gi >= 2:
                            tensor.wait_ge(S["X2E"], gi - 1)
                            tensor.wait_ge(S["X2F"], gi - 1)
                        for dcf in range(NDCF):
                            tgt = pa if dcf < 8 else pb_
                            ins = tensor.transpose(
                                tgt[:, (dcf % 8) * 128:(dcf % 8 + 1) * 128],
                                x2t[:, 0, dcf * 128:(dcf + 1) * 128],
                                ident_s[:])
                        ins.then_inc(S["X2T"], 1)
                    tensor.wait_ge(S["X2E"], gof + NTIL)
                    tensor.wait_ge(S["X2F"], gof + NTIL)
                    if not FFN_GATE:
                        gate_skip = True
                    for tt in (range(2) if FFN_GATE else []):
                        for c in range(NDCF):
                            ins = tensor.matmul(
                                P2[0:8, tt * 512:(tt + 1) * 512], gwT_s[:, c, :],
                                acc[:, c, tt * 512:(tt + 1) * 512],
                                start=(c == 0), stop=(c == NDCF - 1))
                    if FFN_GATE:
                        ins.then_inc(S["GLM"], 1)
                    for tt in (range(2) if FFN_GATE else []):
                        for c in range(NDCF):
                            ins = tensor.matmul(
                                P0[0:1, tt * 512:(tt + 1) * 512], sgw_s[:, c:c + 1],
                                acc[:, c, tt * 512:(tt + 1) * 512],
                                start=(c == 0), stop=(c == NDCF - 1))
                    ins.then_inc(S["SGM"], 1)
                    tensor.wait_ge(S["GLE"], 2 * g + 1)
                    for tt in range(NTIL):
                        ins = tensor.transpose(P3[:, tt * 8:(tt + 1) * 8],
                                               lg_fm[0:8, tt * 128:(tt + 1) * 128],
                                               ident_s[0:8, 0:8])
                    ins.then_inc(S["GTT"], 1)
                    tensor.wait_ge(S["CMB"], g + 1)
                    for tt in range(NTIL):
                        ins = tensor.transpose(P2[0:8, tt * 128:(tt + 1) * 128],
                                               carg[:, tt, :], ident_s[:])
                    ins.then_inc(S["GTT"], 1)
                    tensor.wait_ge(S["SGE"], 2 * g + 1)
                    tensor.wait_ge(S["GLE"], 2 * g + 2)
                    for h2 in range(2):
                        ins = tensor.matmul(
                            P3[:, h2 * 512:(h2 + 1) * 512], ones1b_s[:],
                            sg_row[0:1, h2 * 512:(h2 + 1) * 512],
                            start=True, stop=True)
                    ins.then_inc(S["SGM"], 1)
                    for sl in range(NSLOT):
                        je = g * NSLOT + sl
                        if sl < E:
                            tensor.wait_ge(S["CFME"], g + 1)
                            if je >= 1:
                                tensor.wait_ge(S["SIL"], je * NHS)
                            else:
                                tensor.wait_ge(S["SGE"], 2 * g + 1)
                            for h2 in range(2):
                                ins = tensor.matmul(
                                    P0[:, h2 * 512:(h2 + 1) * 512],
                                    sel8_s[:, sl * 128:(sl + 1) * 128],
                                    comb_fm[0:8, h2 * 512:(h2 + 1) * 512],
                                    start=True, stop=True)
                            ins.then_inc(S["CBC"], 1)
                        for hs in range(NHS):
                            j = je * NHS + hs
                            tensor.wait_ge(S["WLG"], 16 * (j + 1))
                            if j >= 1:
                                tensor.wait_ge(S["SIL"], j)
                            if hs == 0 and sl < E:
                                tensor.wait_ge(S["CBE"], g * E + sl + 1)
                            if hs == 0 and sl == 0:
                                tensor.wait_ge(S["SGE"], 2 * g + 2)
                            for tt in range(2):
                                for c in range(NDCF):
                                    ins = tensor.matmul(
                                        P0[:, tt * 512:(tt + 1) * 512],
                                        wgb[:, 0, c * 128:(c + 1) * 128],
                                        x2b[:, c, tt * 512:(tt + 1) * 512],
                                        start=(c == 0), stop=(c == NDCF - 1))
                            ins.then_inc(S["GUM"], 1)
                            tensor.wait_ge(S["WLU"], 16 * (j + 1))
                            if j >= 1:
                                tensor.wait_ge(S["HT1"], j)
                            for tt in range(2):
                                for c in range(NDCF):
                                    ins = tensor.matmul(
                                        P1[:, tt * 512:(tt + 1) * 512],
                                        wub[:, 0, c * 128:(c + 1) * 128],
                                        x2b[:, c, tt * 512:(tt + 1) * 512],
                                        start=(c == 0), stop=(c == NDCF - 1))
                            ins.then_inc(S["UUM"], 1)
                        tensor.wait_ge(S["WLD"], 16 * (je * NHS + 6))
                        tensor.wait_ge(S["HPD"], je * NHS + 6)
                        for pas in range(2):
                            if pas == 1:
                                tensor.wait_ge(S["WLD"], 16 * (je + 1) * NHS)
                                tensor.wait_ge(S["HPD"], (je + 1) * NHS)
                            hs_list = ([0, 1, 2, 3, 4, 5] if pas == 0
                                       else [6, 7, 8, 9, 10])
                            if sl == 0 and pas == 0:
                                tensor.wait_ge(S["CFME"], g + 1)
                                tensor.wait_ge(S["SGE"], 2 * (g + 1))
                            for dcf in range(NDCF):
                                jd = (je * 2 + pas) * NDCF + dcf
                                if jd >= 2:
                                    tensor.wait_ge(S["DAC"], jd - 1)
                                tgt = P2 if dcf % 2 == 0 else P3
                                for tt in range(2):
                                    for ii, hs in enumerate(hs_list):
                                        slot = hs if hs <= 5 else hs - 6
                                        ins = tensor.matmul(
                                            tgt[:, tt * 512:(tt + 1) * 512],
                                            wde[:, slot,
                                                dcf * 128:(dcf + 1) * 128],
                                            hpb[:, hs, tt * 512:(tt + 1) * 512],
                                            start=(ii == 0),
                                            stop=(ii == len(hs_list) - 1))
                                ins.then_inc(S["DNM"], 1)
                    tensor.wait_ge(S["DAC"], (g + 1) * NSLOT * 2 * NDCF)
                    for i in range(NTIL):
                        gi = gof + i
                        if gi >= 1:
                            tensor.wait_ge(S["OEV"], 2 * gi)
                        for dcf in range(NDCF):
                            tgt = P0 if dcf < 8 else P1
                            ins = tensor.transpose(
                                tgt[:, (dcf % 8) * 128:(dcf % 8 + 1) * 128],
                                acc[:, dcf, i * 128:(i + 1) * 128], ident_s[:])
                        ins.then_inc(S["OTR"], 1)

        @block.vector
        def _(vector):
            mix = [0]

            def vop(ins):
                mix[0] += 1
                ins.then_inc(S["MIX"], 1)

            def vwait():
                vector.wait_ge(S["MIX"], mix[0])

            vector.wait_ge(S["LD"], NLD)
            parc = [0]

            def pwait(ins, bit):
                if PARANOID & (1 << bit):
                    parc[0] += 1
                    ins.then_inc(S["PAR"], 1)
                    vector.wait_ge(S["PAR"], parc[0])
                return ins
            if do_m:
                vector.wait_ge(S["SET"], 1)
                vector.tensor_scalar(out=acol_s[:], in0=acol_s[:], scalar1=-1.0,
                                     scalar2=None, op0=OP.mult).then_inc(S["SET"], 1)
                vector.tensor_scalar(out=opd_row[:], in0=opd_row[:], scalar1=1.0,
                                     scalar2=None, op0=OP.add).then_inc(S["SET"], 1)
                for k in range(NCH):
                    for h in range(2):
                        kh = 2 * k + h
                        for dc in range(NDC):
                            i = kd(k, dc)
                            ih = kh * NDC + dc
                            vector.wait_ge(S["DBX"], ih + 1)
                            vector.wait_ge(S["DAS"], ih + 1)
                            for n in range(NH):
                                ng = h * NH + n
                                init = 0.0 if k == 0 else \
                                    state_s[:, dc, ng:ng + 1]
                                ins = pwait(vector.tensor_tensor_scan(
                                    out=hsc[:, n, :], data0=dA[:, n, :],
                                    data1=dbx[:, dc % 2, n * TC:(n + 1) * TC],
                                    initial=init,
                                    op0=OP.mult, op1=OP.add), 3)
                            ins.then_inc(S["SCN"], 1)
                            fine = (h == 0 and dc == 0)
                            vector.wait_ge(S["BCE"],
                                           8 * kh + (5 if fine else 8))
                            if k >= 1 and dc == 0 and h == 0:
                                vector.wait_ge(S["ZTR"], 4 * k)
                            if h == 0:
                                vector.tensor_mul(out=yac[:, dc, :],
                                                  in0=hsc[:, 0, :],
                                                  in1=crep[:, 0, :])
                                planes = list(range(1, NH))
                            else:
                                planes = list(range(NH))

                            def ym(j):
                                n = planes[j]
                                if fine and n % 2 == 0:
                                    vector.wait_ge(S["BCE"],
                                                   8 * kh + 5 + n // 2)
                                vector.tensor_mul(out=ytmp[:, j % 4, :],
                                                  in0=hsc[:, n, :],
                                                  in1=crep[:, n, :])
                            ym(0)
                            ym(1)
                            for j in range(len(planes)):
                                if j + 2 < len(planes):
                                    ym(j + 2)
                                if j == len(planes) - 1:
                                    vector.tensor_copy(
                                        out=state_s[:, dc,
                                                    h * NH:(h + 1) * NH],
                                        in_=hsc[:, :, TC - 1])
                                parc[0] += 1
                                vector.tensor_add(out=yac[:, dc, :],
                                                  in0=yac[:, dc, :],
                                                  in1=ytmp[:, j % 4, :]
                                                  ).then_inc(S["PAR"], 1)
                                vector.wait_ge(S["PAR"], parc[0])
                            vector.sem_inc(S["YDC"], 1)
                    for ts_ in range(4):
                        it = kt(k, ts_)
                        vector.wait_ge(S["ZTR"], it + 1)
                        vector.wait_ge(S["WTM"], it + 1)
                        src = P1 if ts_ % 2 == 0 else P2
                        vector.scalar_tensor_tensor(
                            out=x_tm[:, ts_, :], in0=src[:], scalar=1.0,
                            in1=wtm[:, ts_, :], op0=OP.mult,
                            op1=OP.add).then_inc(S["ZST"], 1)
            if do_f:
                if fused:
                    vector.wait_ge(S["PGO"], 1)
                inv_d = 1.0 / D

                def ln_micro():
                    vop(vector.tensor_scalar(out=stat[:, 2:3], in0=stat[:, 0:1],
                                             scalar1=inv_d, scalar2=None,
                                             op0=OP.mult))
                    vop(vector.tensor_scalar(out=stat[:, 3:4], in0=stat[:, 1:2],
                                             scalar1=inv_d, scalar2=None,
                                             op0=OP.mult))
                    vwait()
                    vop(vector.tensor_scalar(out=stat[:, 4:5], in0=stat[:, 2:3],
                                             scalar1=stat[:, 2:3], scalar2=None,
                                             op0=OP.mult))
                    vwait()
                    vop(vector.tensor_sub(out=stat[:, 4:5], in0=stat[:, 3:4],
                                          in1=stat[:, 4:5]))
                    vwait()
                    vop(vector.tensor_scalar(out=stat[:, 4:5], in0=stat[:, 4:5],
                                             scalar1=EPS, scalar2=None,
                                             op0=OP.add))
                    vwait()
                    sqrt_wait.append(mix[0])
                    vector.wait_ge(S["MICS"], len(sqrt_wait))
                    vop(vector.reciprocal(out=stat[:, 6:7], in_=stat[:, 5:6]))
                    vwait()

                for g in range(NGRP):
                    gof = g * NTIL
                    for i in range(NTIL):
                        gi = gof + i
                        vector.wait_ge(S["ZLD"], 16 * (gi + 1))
                        vop(vector.reduce_sum(out=stat[:, 0:1],
                                              in_=ztile[:, gi % 2, :],
                                              axis=AX.X))
                        vector.wait_ge(S["LNS"], gi + 1)
                        vwait()
                        ln_micro()
                        if gi >= 1:
                            vector.wait_ge(S["X2T"], gi)
                        vector.tensor_scalar(out=x2t[:, 0, :],
                                             in0=ztile[:, gi % 2, :],
                                             scalar1=stat[:, 2:3],
                                             scalar2=stat[:, 6:7],
                                             op0=OP.subtract,
                                             op1=OP.mult).then_inc(S["LNN"], 1)
                        vector.wait_ge(S["X2T"], gi + 1)
                        vector.wait_ge(S["X2E"], gi + 1)
                        pa, pb_ = (P0, P1) if i % 2 == 0 else (P2, P3)
                        vector.tensor_copy(out=acc[:, 0:8, i * 128:(i + 1) * 128],
                                           in_=pa[:].rearrange(
                                               "p (c h) -> p c h", c=8))
                        vector.tensor_copy(out=acc[:, 8:16, i * 128:(i + 1) * 128],
                                           in_=pb_[:].rearrange(
                                               "p (c h) -> p c h", c=8)
                                           ).then_inc(S["X2F"], 1)
                    vector.wait_ge(S["GLE"], 2 * g + 2)
                    for tt in range(NTIL):
                        lgv = lgT[:, tt, :]
                        vop(vector.reduce_max(out=stat[:, 8:9], in_=lgv,
                                              axis=AX.X))
                        vwait()
                        vop(vector.tensor_scalar(out=cmask[:, tt, :], in0=lgv,
                                                 scalar1=stat[:, 8:9],
                                                 scalar2=-1e30,
                                                 op0=OP.is_ge, op1=OP.mult))
                        vwait()
                        vop(vector.tensor_add(out=carg[:, tt, :], in0=lgv,
                                              in1=cmask[:, tt, :]))
                        vwait()
                        vop(vector.reduce_max(out=stat[:, 9:10],
                                              in_=carg[:, tt, :], axis=AX.X))
                        vwait()
                        vop(vector.tensor_scalar(out=cmask[:, tt, :], in0=lgv,
                                                 scalar1=stat[:, 9:10],
                                                 scalar2=None, op0=OP.is_ge))
                        vop(vector.tensor_tensor(out=stat[:, 10:11],
                                                 in0=stat[:, 8:9],
                                                 in1=stat[:, 9:10], op=OP.add))
                        vwait()
                        vop(vector.tensor_scalar(out=carg[:, tt, :], in0=lgv,
                                                 scalar1=2.0,
                                                 scalar2=stat[:, 10:11],
                                                 op0=OP.mult, op1=OP.subtract))
                        vwait()
                    vector.sem_inc(S["TOP"], 1)
                    vector.wait_ge(S["SIG"], g + 1)
                    vector.tensor_mul(out=carg[:], in0=carg[:],
                                      in1=cmask[:]).then_inc(S["CMB"], 1)
                    if g >= 1:
                        vector.wait_ge(S["DNM"], g * NSLOT * 2 * NDCF)
                    for sl in range(NSLOT):
                        je = g * NSLOT + sl
                        if sl < E:
                            vector.wait_ge(S["CBE"], g * E + sl + 1)
                        else:
                            vector.wait_ge(S["SGE"], 2 * g + 2)
                        for hs in range(NHS):
                            j = je * NHS + hs
                            vector.wait_ge(S["SIL"], j + 1)
                            vector.wait_ge(S["UUM"], j + 1)
                            vector.tensor_mul(out=hpt[:], in0=outt[:, 0:GTOK],
                                              in1=P1[:]).then_inc(S["HT1"], 1)
                            vector.wait_ge(S["HT1"], j + 1)
                            scl = sg_rep[:] if sl >= E else comb_e[:, sl % 2, :]
                            vector.tensor_mul(out=hpb[:, hs, :], in0=hpt[:],
                                              in1=scl).then_inc(S["HPD"], 1)
                            vector.wait_ge(S["HPD"], j + 1)
                        for pas in range(2):
                            for dcf in range(NDCF):
                                jd = (je * 2 + pas) * NDCF + dcf
                                vector.wait_ge(S["DNM"], jd + 1)
                                src = P2 if dcf % 2 == 0 else P3
                                vector.tensor_add(out=acc[:, dcf, :],
                                                  in0=acc[:, dcf, :],
                                                  in1=src[:]).then_inc(S["DAC"], 1)
                    vector.wait_ge(S["ZLD"], 16 * NTIL * (g + 1))
                    for i in range(NTIL):
                        gi = gof + i
                        vector.wait_ge(S["LN2S"], gi + 1)
                        vop(vector.reduce_sum(out=stat[:, 0:1], in_=outt[:],
                                              axis=AX.X))
                        vwait()
                        ln_micro()
                        if gi >= 2:
                            vector.wait_ge(S["ODM"], 16 * (gi - 1))
                        vector.tensor_scalar(out=ztile[:, gi % 2, :], in0=outt[:],
                                             scalar1=stat[:, 2:3],
                                             scalar2=stat[:, 6:7],
                                             op0=OP.subtract,
                                             op1=OP.mult).then_inc(S["LN2N"], 1)

        @block.scalar
        def _(scalar):
            scalar.wait_ge(S["LD"], NLD)
            if do_m:
                scalar.activation(out=acol_s[:], in_=acol_s[:],
                                  func=AF.Exp).then_inc(S["SET"], 1)
                scalar.wait_ge(S["SET"], 4)
                scalar.copy(out=opd_rep[:], in_=P1[:]).then_inc(S["SET"], 1)
                for k in range(NCH):
                    for ts_ in range(4):
                        it = 4 * k + ts_
                        scalar.wait_ge(S["M0T"], it + 1)
                        sv = PB[3 - (ts_ % 2)][:].bitcast(BF16)
                        scalar.copy(
                            out=m0_xf[:, :, ts_ * 128:(ts_ + 1) * 128],
                            in_=sv.rearrange("p (c h) -> p c h", c=NDCF),
                        ).then_inc(S["M0E"], 1)
                    scalar.wait_ge(S["M0M"], k + 1)
                    scalar.copy(out=draw_s[:, k * TC:(k + 1) * TC],
                                in_=P0[0:64, 0:512])
                    scalar.copy(out=bm_s[:, k * TC:(k + 1) * TC],
                                in_=P0[64:80, 0:512])
                    scalar.copy(out=cm_s[:, k * TC:(k + 1) * TC],
                                in_=P1[0:16, 0:512]).then_inc(S["M0V"], 1)
                for k in range(NCH):
                    for ts_ in range(4):
                        it = kt(k, ts_)
                        scalar.wait_ge(S["UTR"], it + 1)
                        half = (ts_ % 2) * 1024
                        scalar.copy(
                            out=xfmh[:, :, ts_ * 128:(ts_ + 1) * 128],
                            in_=P3[:].bitcast(BF16)[:, half:half + 1024]
                            .rearrange("p (c h) -> p c h", c=NDC),
                        ).then_inc(S["UEV"], 1)

                    def bc_evacs(h, pieces=range(8)):
                        kh = 2 * k + h
                        for p in pieces:
                            scalar.wait_ge(S["BCM"], 8 * kh + p + 1)
                            if kh >= 1:
                                scalar.wait_ge(S["SCN"] if p < 4 else S["YDC"],
                                               NDC * kh)
                            src = P1 if p % 2 == 0 else P2
                            dstt = brep if p < 4 else crep
                            pp = p % 4
                            scalar.copy(
                                out=dstt[:, 2 * pp:2 * pp + 2, :],
                                in_=src[:].rearrange("p (a t) -> p a t", a=2),
                            ).then_inc(S["BCE"], 1)

                    def exp_dc(h, dc):
                        i = kd(k, dc)
                        ih = (2 * k + h) * NDC + dc
                        if h == 0:
                            scalar.wait_ge(S["SPP"], i + 1)
                        if ih >= 1:
                            scalar.wait_ge(S["SCN"], ih)
                        for n in range(NH):
                            ng = h * NH + n
                            ins = scalar.activation(
                                out=dA[:, n, :], in_=delta_s[:, dc, :],
                                func=AF.Exp,
                                scale=acol_s[:, dc * N + ng:dc * N + ng + 1])
                        ins.then_inc(S["DAS"], 1)

                    bc_evacs(0, range(0, 4))
                    for dc in range(NDC):
                        i = kd(k, dc)
                        scalar.wait_ge(S["DMM"], i + 1)
                        if i >= 2:
                            scalar.wait_ge(S["SPP"], i - 1)
                        ps = P0[:, (dc % 2) * 512:(dc % 2 + 1) * 512]
                        bb = bdt_s[:, dc:dc + 1]
                        scalar.activation(out=sp1[:, i % 2, :], in_=ps,
                                          func=AF.Abs, bias=bb)
                        scalar.activation(out=sp2[:, i % 2, :], in_=ps,
                                          func=AF.Relu,
                                          bias=bb).then_inc(S["SPS"], 1)
                        scalar.activation(out=sp1[:, i % 2, :],
                                          in_=sp1[:, i % 2, :],
                                          func=AF.Exp, scale=-1.0)
                        scalar.activation(out=sp1[:, i % 2, :],
                                          in_=sp1[:, i % 2, :],
                                          func=AF.Ln,
                                          bias=1.0).then_inc(S["SPA"], 1)
                        exp_dc(0, dc)
                        if dc == 0:
                            bc_evacs(0, [4, 5])
                        elif dc == 1:
                            bc_evacs(0, [6, 7])
                    bc_evacs(1)
                    for dc in range(NDC):
                        exp_dc(1, dc)
            if do_f:
                if fused:
                    scalar.wait_ge(S["PGO"], 1)
                    rzp = scalar.alloc_register("rzp")
                    scalar.load(rzp, cinfo_s[0:1, 0:1])
                    zdhv = scalar.snap(rzp, min_val=0, max_val=1)
                    zsh_ld = ap["zsh"].rearrange(
                        "(u v w p) d -> v u w p d", p=128, w=4, v=2)[zdhv]
                q = 0
                for g in range(NGRP):
                    gof = g * NTIL
                    if g >= 1:
                        scalar.wait_ge(S["GUM"], g * NSLOT * NHS)
                        scalar.wait_ge(S["UUM"], g * NSLOT * NHS)
                    for i in range(NTIL):
                        gi = gof + i
                        if gi >= 2:
                            scalar.wait_ge(S["LNN"], gi - 1)
                        if g >= 1 and i < 2:
                            scalar.wait_ge(S["ODM"], 16 * (NTIL * g - 1 + i))
                        if mode == "ffn":
                            scalar.dma_start(
                                out=ztile[:, gi % 2, :],
                                in_=ap["zin"][gi * 128:(gi + 1) * 128, :],
                            ).then_inc(S["ZLD"], 16)
                        else:
                            scalar.dma_start(
                                out=ztile[:, gi % 2, :],
                                in_=zsh_ld[2 * g + i // 4, i % 4],
                            ).then_inc(S["ZLD"], 16)
                        if g >= 1 and i == 0:
                            scalar.wait_ge(S["LN2N"], NTIL * g)
                        scalar.wait_ge(S["ZLD"], 16 * (gi + 1))
                        scalar.activation(out=outt[:], in_=ztile[:, gi % 2, :],
                                          func=AF.Square,
                                          accum_out=stat[:, 1:2]).then_inc(S["LNS"], 1)
                        scalar.wait_ge(S["MIX"], sqrt_wait[q])
                        scalar.activation(out=stat[:, 5:6], in_=stat[:, 4:5],
                                          func=AF.Sqrt).then_inc(S["MICS"], 1)
                        q += 1
                        scalar.wait_ge(S["X2T"], gi + 1)
                        pa, pb_ = (P0, P1) if i % 2 == 0 else (P2, P3)
                        scalar.copy(out=x2b[:, 0:8, i * 128:(i + 1) * 128],
                                    in_=pa[:].rearrange("p (c h) -> p c h", c=8))
                        scalar.copy(out=x2b[:, 8:16, i * 128:(i + 1) * 128],
                                    in_=pb_[:].rearrange("p (c h) -> p c h", c=8)
                                    ).then_inc(S["X2E"], 1)
                    scalar.wait_ge(S["GLM"], g + 1)
                    scalar.copy(out=lg_fm[:], in_=P2[0:8, :]).then_inc(S["GLE"], 1)
                    scalar.wait_ge(S["SGM"], 2 * g + 1)
                    scalar.activation(out=sg_row[:], in_=P0[0:1, :],
                                      func=AF.Sigmoid).then_inc(S["SGE"], 1)
                    scalar.wait_ge(S["GTT"], 2 * g + 1)
                    scalar.copy(out=lgT[:],
                                in_=P3[:, 0:NTIL * 8]
                                .rearrange("p (t e) -> p t e", t=NTIL)
                                ).then_inc(S["GLE"], 1)
                    scalar.wait_ge(S["TOP"], g + 1)
                    scalar.activation(out=carg[:], in_=carg[:],
                                      func=AF.Sigmoid).then_inc(S["SIG"], 1)
                    scalar.wait_ge(S["GTT"], 2 * g + 2)
                    scalar.copy(out=comb_fm[:], in_=P2[0:8, :]).then_inc(S["CFME"], 1)
                    scalar.wait_ge(S["SGM"], 2 * g + 2)
                    if g >= 1:
                        scalar.wait_ge(S["HPD"], g * NSLOT * NHS)
                    scalar.copy(out=sg_rep[:], in_=P3[:]).then_inc(S["SGE"], 1)
                    for sl in range(NSLOT):
                        je = g * NSLOT + sl
                        if sl < E:
                            scalar.wait_ge(S["CBC"], g * E + sl + 1)
                            if g * E + sl >= 2:
                                scalar.wait_ge(S["HPD"], (je - 1) * NHS)
                            scalar.copy(out=comb_e[:, sl % 2, :],
                                        in_=P0[:]).then_inc(S["CBE"], 1)
                        for hs in range(NHS):
                            j = je * NHS + hs
                            scalar.wait_ge(S["GUM"], j + 1)
                            if j >= 1:
                                scalar.wait_ge(S["HT1"], j)
                            scalar.activation(out=outt[:, 0:GTOK], in_=P0[:],
                                              func=AF.Silu).then_inc(S["SIL"], 1)
                    for i in range(NTIL):
                        gi = gof + i
                        scalar.wait_ge(S["OTR"], gi + 1)
                        if i == 0:
                            scalar.wait_ge(S["X2T"], NTIL * (g + 1))
                        if gi >= 1:
                            scalar.wait_ge(S["LN2N"], gi)
                        scalar.copy(out=outt[:, 0:1024], in_=P0[:]).then_inc(S["OEV"], 1)
                        scalar.copy(out=outt[:, 1024:2048],
                                    in_=P1[:]).then_inc(S["OEV"], 1)
                        scalar.activation(out=x2t[:, 0, :], in_=outt[:],
                                          func=AF.Square,
                                          accum_out=stat[:, 1:2]).then_inc(S["LN2S"], 1)
                        scalar.wait_ge(S["MIX"], sqrt_wait[q])
                        scalar.activation(out=stat[:, 5:6], in_=stat[:, 4:5],
                                          func=AF.Sqrt).then_inc(S["MICS"], 1)
                        q += 1

    return nc



def _sel_matrix(nrows, nsel, width=128):
    m = np.zeros((nrows, nsel * width), ml_dtypes.bfloat16)
    for n in range(nsel):
        if n < nrows:
            m[n, n * width:(n + 1) * width] = 1.0
    return m


def _sel_n():
    m = np.zeros((16, N * 128), ml_dtypes.bfloat16)
    for ng in range(N):
        m[ng, ng * 128:(ng + 1) * 128] = 1.0
    return m


def _sel_8():
    m = np.zeros((8, E * 128), ml_dtypes.bfloat16)
    for e in range(E):
        m[e, e * 128:(e + 1) * 128] = 1.0
    return m


def _prep_core_inputs(c, full, nonce, mode):
    b_, dh = c // 2, c % 2
    d0 = dh * DH
    f32 = np.float32
    out = {
        "ident": np.eye(128, dtype=f32),
        "identb": np.eye(128, dtype=ml_dtypes.bfloat16),
        "ones1": np.ones((1, 128), f32),
        "ones1b": np.ones((1, 128), ml_dtypes.bfloat16),
        "cinfo": np.array([[dh, 1 - dh, nonce, 0, 0, 0, 0, 0]], np.int32),
    }
    if mode in ("mamba", "fused"):
        xb = np.ascontiguousarray(np.asarray(full["L0_attn_x"][b_], f32))
        out["x"] = xb
        out["xh"] = np.ascontiguousarray(xb[:, d0:d0 + DH])
        out["wxT"] = np.ascontiguousarray(np.asarray(full["L0_attn_W_xproj"], f32).T)
        out["wdT"] = np.ascontiguousarray(
            np.asarray(full["L0_attn_W_dt"], f32)[d0:d0 + DH].T)
        out["bdt"] = np.ascontiguousarray(
            np.asarray(full["L0_attn_b_dt"], f32)[d0:d0 + DH].reshape(NDC, 128).T)
        out["alog"] = np.ascontiguousarray(
            np.asarray(full["L0_attn_A_log"], f32)[d0:d0 + DH]
            .reshape(NDC, 128, N).transpose(1, 0, 2).reshape(128, NDC * N))
        out["dprm"] = np.ascontiguousarray(
            np.asarray(full["L0_attn_D_param"], f32).reshape(1, D)[:, d0:d0 + DH])
        out["seln"] = _sel_n()
    if mode in ("ffn", "fused"):
        out["gwT"] = np.ascontiguousarray(np.asarray(full["L0_ffn_gate_w"], f32).T)
        out["sgw"] = np.ascontiguousarray(
            np.asarray(full["L0_ffn_shgate_w"], f32).reshape(NDCF, 128).T)
        out["sel8"] = _sel_8()
        out["wg"] = np.ascontiguousarray(np.asarray(full["L0_ffn_w_gate"], f32))
        out["wu"] = np.ascontiguousarray(np.asarray(full["L0_ffn_w_up"], f32))
        out["wd"] = np.ascontiguousarray(np.asarray(full["L0_ffn_w_down"], f32))
        out["swg"] = np.ascontiguousarray(np.asarray(full["L0_ffn_shw_gate"], f32))
        out["swu"] = np.ascontiguousarray(np.asarray(full["L0_ffn_shw_up"], f32))
        out["swd"] = np.ascontiguousarray(np.asarray(full["L0_ffn_shw_down"], f32))
    if mode == "ffn":
        pass
    return out


def _own_rows(dh):
    return np.concatenate([np.arange(ch * TC, (ch + 1) * TC)
                           for ch in (dh, dh + 2, dh + 4, dh + 6)])


def _scatter_out(res, key):
    out = np.empty((B, L, D), np.float32)
    for c in range(8):
        b_, dh = c // 2, c % 2
        oy = np.asarray(res.results[c][key]).reshape(4, TC, D)
        for j, ch in enumerate((dh, dh + 2, dh + 4, dh + 6)):
            out[b_, ch * TC:(ch + 1) * TC] = oy[j]
    return out


KERNEL_MODE = "fused"


def kernel(**inputs):
    if KERNEL_MODE == "fused":
        nonce = int(np.random.default_rng().integers(1, 2**31 - 1))
        nc = build_program("fused")
        in_maps = [_prep_core_inputs(c, inputs, nonce, "fused") for c in range(8)]
        res = run_bass_kernel_spmd(nc, in_maps, core_ids=list(range(8)))
        return _scatter_out(res, "oy")
    nc_m = build_program("mamba")
    in_maps = [_prep_core_inputs(c, inputs, 0, "mamba") for c in range(8)]
    res_m = run_bass_kernel_spmd(nc_m, in_maps, core_ids=list(range(8)))
    z = np.empty((B, L, D), np.float32)
    for c in range(8):
        b_, dh = c // 2, c % 2
        z[b_, :, dh * DH:(dh + 1) * DH] = np.asarray(res_m.results[c]["zout"])
    nc_f = build_program("ffn")
    in_maps = []
    for c in range(8):
        b_, dh = c // 2, c % 2
        m = _prep_core_inputs(c, inputs, 0, "ffn")
        m["zin"] = np.ascontiguousarray(z[b_][_own_rows(dh)])
        in_maps.append(m)
    res_f = run_bass_kernel_spmd(nc_f, in_maps, core_ids=list(range(8)))
    return _scatter_out(res_f, "oy")


# version 07
# speedup vs baseline: 1.0587x; 1.0587x over previous
import numpy as np
import ml_dtypes
import concourse.bass as bass
import concourse.mybir as mybir
from concourse.bass_utils import run_bass_kernel_spmd

F32 = mybir.dt.float32
BF16 = mybir.dt.bfloat16
I32 = mybir.dt.int32
AF = mybir.ActivationFunctionType
OP = mybir.AluOpType
AX = mybir.AxisListType

B, L, D, N, R = 4, 4096, 2048, 16, 64
DH = D // 2
TC = 512
NCH = L // TC
NDC = DH // 128
NDCF = D // 128
NH = N // 2
E, H, HS = 8, 1408, 2816
NHS = H // 128
WDS = 6
NTOK = L // 2
NGRP = 2
GTOK = NTOK // NGRP
NTIL = GTOK // 128
EPS = 1e-5
NSLOT = 10
ABASE = 17472
PARANOID = 0
FFN_GATE = True


def build_program(mode):
    assert mode in ("mamba", "ffn", "fused")
    do_m = mode in ("mamba", "fused")
    do_f = mode in ("ffn", "fused")
    fused = mode == "fused"

    nc = bass.Bass("TRN2", target_bir_lowering=False, debug=False)
    ap = {}

    def din(name, shape, dt=F32):
        ap[name] = nc.dram_tensor(name, shape, dt, kind="ExternalInput").ap()

    def dout(name, shape, dt=F32):
        ap[name] = nc.dram_tensor(name, shape, dt, kind="ExternalOutput").ap()

    din("ident", [128, 128]); din("identb", [128, 128], BF16)
    din("ones1", [1, 128]); din("ones1b", [1, 128], BF16)
    din("cinfo", [1, 8], I32)
    if do_m:
        din("x", [L, D]); din("xh", [L, DH])
        din("wxT", [D, 96]); din("wdT", [R, DH])
        din("bdt", [128, NDC]); din("alog", [128, NDC * N]); din("dprm", [1, DH])
        din("seln", [16, N * 128], BF16)
    if do_f:
        din("gwT", [D, E]); din("sgw", [128, NDCF])
        din("sel8", [8, E * 128], BF16)
        din("wg", [E, D, H]); din("wu", [E, D, H]); din("wd", [E, H, D])
        din("swg", [D, HS]); din("swu", [D, HS]); din("swd", [HS, D])
        dout("oy", [NTOK, D])
        wgsc = nc.dram_tensor("wgsc", [NSLOT * NHS, 128, NDCF * 128], BF16,
                              kind="Internal").ap()
        wusc = nc.dram_tensor("wusc", [NSLOT * NHS, 128, NDCF * 128], BF16,
                              kind="Internal").ap()
        wdsc = nc.dram_tensor("wdsc", [NSLOT * NHS, 128, D], BF16,
                              kind="Internal").ap()
    if mode == "mamba":
        dout("zout", [L, DH])
    elif mode == "ffn":
        din("zin", [NTOK, D])
    else:
        ap["zsh"] = nc.dram_tensor("zsh", [L, D], F32, addr_space="Shared").ap()
        ap["flg"] = nc.dram_tensor("flg", [2, 16], I32, addr_space="Shared").ap()

    pos = [ABASE]
    sb_off = {}

    def sb(name, shape, dtype, at=None):
        free = int(np.prod(shape[1:]))
        nbytes = free * (2 if dtype == BF16 else 4)
        if at is None:
            at = pos[0]
            pos[0] += (nbytes + 63) & ~63
        else:
            pos[0] = max(pos[0], at + ((nbytes + 63) & ~63))
        sb_off[name] = at
        return nc.alloc_sbuf_tensor_at(name, [shape[0]] + list(shape[1:]),
                                       dtype, offset=at)

    ident_s = sb("ident_s", [128, 128], F32)
    identb_s = sb("identb_s", [128, 128], BF16)
    ones1_s = sb("ones1_s", [1, 128], F32)
    ones1b_s = sb("ones1b_s", [1, 128], BF16)
    cinfo_s = sb("cinfo_s", [1, 8], I32)
    pollt = sb("pollt", [1, 4], I32)
    if do_f:
        gwT_s = sb("gwT_s", [128, NDCF, E], F32)
        sgw_s = sb("sgw_s", [128, NDCF], F32)
        sel8_s = sb("sel8_s", [8, E * 128], BF16)
    common_end = pos[0]

    if do_m:
        pos[0] = common_end
        wxT_s = sb("wxT_s", [128, NDCF, 96], BF16)
        wdT_s = sb("wdT_s", [64, DH], BF16)
        bdt_s = sb("bdt_s", [128, NDC], F32)
        acol_s = sb("acol_s", [128, NDC * N], F32)
        opd_row = sb("opd_row", [1, DH], F32)
        opd_rep = sb("opd_rep", [128, DH], F32)
        seln_s = sb("seln_s", [16, N * 128], BF16)
        draw_s = sb("draw_s", [64, L], BF16)
        bm_s = sb("bm_s", [16, L], BF16)
        cm_s = sb("cm_s", [16, L], BF16)
        state_s = sb("state_s", [128, NDC, N], F32)
        x_tm = sb("x_tm", [128, 4, DH], F32)
        x_tmb = sb("x_tmb", [128, 4, DH], BF16)
        xfmh = sb("xfmh", [128, NDC, TC], BF16)
        delta_s = sb("delta_s", [128, NDC, TC], F32)
        u_all = sb("u_all", [128, NDC, TC], BF16)
        dA = sb("dA", [128, NH, TC], F32)
        dbx = sb("dbx", [128, 2, NH * TC], BF16)
        hsc = sb("hsc", [128, NH, TC], BF16)
        yac = sb("yac", [128, NDC, TC], F32)
        wtm = sb("wtm", [128, 4, DH], F32)
        brep = sb("brep", [128, NH, TC], BF16)
        crep = sb("crep", [128, NH, TC], BF16)
        ytmp = sb("ytmp", [128, 4, TC], BF16)
        sp1 = sb("sp1", [128, 2, TC], F32)
        sp2 = sb("sp2", [128, 2, TC], F32)
        m0_xb = sb("m0_xb", [128, 4, D], BF16, at=sb_off["dA"])
        m0_xf = sb("m0_xf", [128, NDCF, TC], BF16, at=sb_off["yac"])
        m_end = pos[0]
    else:
        m_end = common_end

    if do_f:
        pos[0] = common_end
        acc = sb("acc", [128, NDCF, GTOK], F32)
        x2b = sb("x2b", [128, NDCF, GTOK], BF16)
        hpb = sb("hpb", [128, NHS, GTOK], BF16)
        wde = sb("wde", [128, 6, D], BF16)
        wgb = sb("wgb", [128, 1, NDCF * 128], BF16)
        wub = sb("wub", [128, 1, NDCF * 128], BF16)
        comb_e = sb("comb_e", [128, 2, GTOK], BF16)
        sg_rep = sb("sg_rep", [128, GTOK], BF16)
        ztile = sb("ztile", [128, 2, D], F32)
        x2t = sb("x2t", [128, 1, D], F32)
        outt = sb("outt", [128, D], F32)
        hpt = sb("hpt", [128, GTOK], BF16)
        stat = sb("stat", [128, 16], F32)
        lg_fm = sb("lg_fm", [8, GTOK], F32)
        lgT = sb("lgT", [128, NTIL, E], F32)
        carg = sb("carg", [128, NTIL, E], F32)
        cmask = sb("cmask", [128, NTIL, E], F32)
        comb_fm = sb("comb_fm", [8, GTOK], BF16)
        sg_row = sb("sg_row", [1, GTOK], BF16)
        f_end = pos[0]
    else:
        f_end = common_end

    assert max(m_end, f_end) <= 229376, (m_end, f_end)

    P0 = nc.alloc_psum_tensor("P0", [128, 1024], F32)
    P1 = nc.alloc_psum_tensor("P1", [128, 1024], F32)
    P2 = nc.alloc_psum_tensor("P2", [128, 1024], F32)
    P3 = nc.alloc_psum_tensor("P3", [128, 1024], F32)
    PB = [P0, P1, P2, P3]

    S = {}
    for nm in ["LD", "SET",
               "M0X", "M0T", "M0E", "M0M", "M0V",
               "XTF", "XTB", "UTR", "UEV", "DMM", "SPS", "SPA", "SPP",
               "BCM", "BCE", "TSA", "DAS", "SCN", "YDC", "ZTR", "WTM", "ZST",
               "ZDM", "FLG", "PGO", "PRT",
               "ZLD", "LNS", "LNN", "X2T", "X2E", "X2F",
               "GLM", "GLE", "GTT", "TOP", "SIG", "CMB",
               "CBC", "CBE", "SGM", "SGE",
               "WLG", "WLU", "WLD", "GUM", "UUM", "SIL", "HT1", "HPD",
               "DNM", "DAC", "OTR", "OEV", "LN2S", "LN2N", "ODM",
               "MIX", "MICS", "CFME", "LDG", "PAR", "DBX", "CAST"]:
        S[nm] = nc.alloc_semaphore(nm)

    setup = [("ident", ident_s[:], None), ("identb", identb_s[:], None),
             ("ones1", ones1_s[:], None), ("ones1b", ones1b_s[:], None),
             ("cinfo", cinfo_s[:], None)]
    if do_m:
        setup += [("bdt", bdt_s[:], None),
                  ("alog", acol_s[:], None), ("dprm", opd_row[:], None),
                  ("seln", seln_s[:], None)]
    if do_f:
        setup += [
            ("gwT", gwT_s[:], ap["gwT"].rearrange("(c p) e -> p c e", p=128)),
            ("sgw", sgw_s[:], None),
            ("sel8", sel8_s[:], None),
        ]
    NLD = len(setup) * 16

    cast_jobs = []
    if do_f:
        for sl in range(NSLOT):
            shared = sl >= E
            for hs in range(NHS):
                hh = hs if not shared else (sl - E) * NHS + hs
                js = sl * NHS + hs
                if shared:
                    srcg = ap["swg"][:, hh * 128:(hh + 1) * 128]
                    srcu = ap["swu"][:, hh * 128:(hh + 1) * 128]
                    srcd = ap["swd"][hh * 128:(hh + 1) * 128, :]
                else:
                    srcg = ap["wg"][sl, :, hh * 128:(hh + 1) * 128]
                    srcu = ap["wu"][sl, :, hh * 128:(hh + 1) * 128]
                    srcd = ap["wd"][sl, hh * 128:(hh + 1) * 128, :]
                cast_jobs.append(
                    (wgsc[js].rearrange("p (c h) -> p c h", c=NDCF),
                     srcg.rearrange("(c p) h -> p c h", p=128)))
                cast_jobs.append(
                    (wusc[js].rearrange("p (c h) -> p c h", c=NDCF),
                     srcu.rearrange("(c p) h -> p c h", p=128)))
                cast_jobs.append((wdsc[js], srcd))

    def kd(k, dc):
        return k * NDC + dc

    def kt(k, ts_):
        return k * 4 + ts_

    sqrt_wait = []

    with nc.Block() as block:

        @block.sync
        def _(sync):
            for name, dst, src in setup:
                sync.dma_start(out=dst, in_=(src if src is not None else ap[name])
                               ).then_inc(S["LD"], 16)
            if fused:
                rpar = sync.alloc_register("rpar")
                sync.wait_ge(S["LD"], 16 * 5)
                sync.load(rpar, cinfo_s[0:1, 0:1])
                dhv = sync.snap(rpar, min_val=0, max_val=1)
                zsh_st = ap["zsh"].rearrange(
                    "(c s p) (h d) -> h c p s d", p=128, s=4, h=2)[dhv]

            def zstore(k):
                src = x_tm[:]
                if mode == "mamba":
                    dst = ap["zout"][k * TC:(k + 1) * TC, :] \
                        .rearrange("(s p) d -> p s d", p=128)
                else:
                    dst = zsh_st[k]
                sync.dma_start(out=dst, in_=src).then_inc(S["ZDM"], 16)

            if do_m:
                for k in range(NCH):
                    if k >= 1:
                        sync.wait_ge(S["ZST"], 4 * k)
                        zstore(k - 1)
                        sync.wait_ge(S["ZDM"], 16 * k)
                    sync.dma_start(
                        out=x_tm[:],
                        in_=ap["xh"][k * TC:(k + 1) * TC, :]
                        .rearrange("(s p) d -> p s d", p=128),
                    ).then_inc(S["XTF"], 16)
                sync.wait_ge(S["ZST"], 4 * NCH)
                zstore(NCH - 1)
            if fused:
                sync.wait_ge(S["ZDM"], 16 * NCH)
                sync.dma_start(out=ap["flg"][dhv, 0:1],
                               in_=cinfo_s[0:1, 2:3]).then_inc(S["FLG"], 16)
            if do_f:
                def wgu_loads(j0, j1):
                    for j in range(j0, j1):
                        je, hs = divmod(j, NHS)
                        sl = je % NSLOT
                        js = sl * NHS + hs
                        sync.wait_ge(S["CAST"], 16 * (3 * js + 1))
                        if j >= 1:
                            sync.wait_ge(S["GUM"], j)
                        sync.dma_start(out=wgb[:, 0, :],
                                       in_=wgsc[js]).then_inc(S["WLG"], 16)
                        sync.wait_ge(S["CAST"], 16 * (3 * js + 2))
                        if j >= 1:
                            sync.wait_ge(S["UUM"], j)
                        sync.dma_start(out=wub[:, 0, :],
                                       in_=wusc[js]).then_inc(S["WLU"], 16)

                def ostores(g):
                    for i in range(NTIL):
                        gi = g * NTIL + i
                        sync.wait_ge(S["LN2N"], gi + 1)
                        sync.dma_start(out=ap["oy"][gi * 128:(gi + 1) * 128, :],
                                       in_=ztile[:, gi % 2, :]
                                       ).then_inc(S["ODM"], 16)

                wgu_loads(0, NSLOT * NHS + 1)
                ostores(0)
                wgu_loads(NSLOT * NHS + 1, NGRP * NSLOT * NHS)
                ostores(1)
                sync.wait_ge(S["ODM"], 16 * NGRP * NTIL)
            elif mode == "mamba":
                sync.wait_ge(S["ZDM"], 16 * NCH)

        @block.gpsimd
        def _(gpsimd):
            cast_pos = [0]

            def emit_casts(n):
                for _ in range(n):
                    if cast_pos[0] >= len(cast_jobs):
                        return
                    d, s = cast_jobs[cast_pos[0]]
                    gpsimd.dma_start(out=d, in_=s).then_inc(S["CAST"], 16)
                    cast_pos[0] += 1
            if do_m:
                gpsimd.dma_start(
                    out=wxT_s[:],
                    in_=ap["wxT"].rearrange("(c p) e -> p c e", p=128),
                ).then_inc(S["LDG"], 16)
                gpsimd.dma_start(out=wdT_s[:], in_=ap["wdT"]).then_inc(S["LDG"], 16)
                for k in range(NCH):
                    if k >= 1:
                        gpsimd.wait_ge(S["M0T"], 4 * k)
                    gpsimd.dma_start(
                        out=m0_xb[:],
                        in_=ap["x"][k * TC:(k + 1) * TC, :]
                        .rearrange("(s p) d -> p s d", p=128),
                    ).then_inc(S["M0X"], 16)
                for k in range(NCH):
                    if k >= 1:
                        gpsimd.wait_ge(S["UTR"], 4 * k)
                    gpsimd.dma_start(
                        out=x_tmb[:],
                        in_=ap["xh"][k * TC:(k + 1) * TC, :]
                        .rearrange("(s p) d -> p s d", p=128),
                    ).then_inc(S["XTB"], 16)
                    def dbx_dc(h, dc):
                        kh = 2 * k + h
                        ih = kh * NDC + dc
                        gpsimd.wait_ge(S["BCE"], 8 * kh + 4)
                        if ih >= 2:
                            gpsimd.wait_ge(S["SCN"], ih - 1)
                        for n in range(NH):
                            ins = gpsimd.tensor_mul(
                                out=dbx[:, dc % 2, n * TC:(n + 1) * TC],
                                in0=u_all[:, dc, :], in1=brep[:, n, :])
                        ins.then_inc(S["DBX"], 1)
                        emit_casts(3)

                    for dc in range(NDC):
                        i = kd(k, dc)
                        gpsimd.wait_ge(S["SPA"], i + 1)
                        gpsimd.tensor_add(out=delta_s[:, dc, :],
                                          in0=sp1[:, i % 2, :],
                                          in1=sp2[:, i % 2, :]).then_inc(S["SPP"], 1)
                        gpsimd.wait_ge(S["UEV"], 4 * (k + 1))
                        gpsimd.tensor_mul(out=u_all[:, dc, :],
                                          in0=delta_s[:, dc, :],
                                          in1=xfmh[:, dc, :])
                        dbx_dc(0, dc)
                    for dc in range(NDC):
                        dbx_dc(1, dc)
                    gpsimd.wait_ge(S["SET"], 5)
                    gpsimd.wait_ge(S["XTF"], 16 * (k + 1))
                    for ts_ in range(4):
                        if k >= 1:
                            gpsimd.wait_ge(S["ZST"], kt(k - 1, ts_) + 1)
                        gpsimd.tensor_mul(out=wtm[:, ts_, :], in0=x_tm[:, ts_, :],
                                          in1=opd_rep[:]).then_inc(S["WTM"], 1)
            if do_f:
                if fused:
                    gpsimd.wait_ge(S["FLG"], 16)
                    rpn = gpsimd.alloc_register("rpn")
                    rnon = gpsimd.alloc_register("rnon")
                    rcnt = gpsimd.alloc_register("rcnt")
                    rval = gpsimd.alloc_register("rval")
                    gpsimd.load(rpn, cinfo_s[0:1, 1:2])
                    gpsimd.load(rnon, cinfo_s[0:1, 2:3])
                    pnv = gpsimd.snap(rpn, min_val=0, max_val=1)
                    gpsimd.reg_mov(rcnt, 0)
                    gpsimd.reg_mov(rval, 0)

                    def cond():
                        a = gpsimd.snap(rval, min_val=-2**31, max_val=2**31 - 1)
                        bq = gpsimd.snap(rnon, min_val=-2**31, max_val=2**31 - 1)
                        return a != bq
                    with gpsimd.While(cond):
                        gpsimd.reg_add(rcnt, rcnt, 16)
                        gpsimd.dma_start(out=pollt[0:1, 0:1],
                                         in_=ap["flg"][pnv, 0:1]
                                         ).then_inc(S["PRT"], 16)
                        gpsimd.wait_ge(S["PRT"], rcnt)
                        gpsimd.load(rval, pollt[0:1, 0:1])
                    gpsimd.sem_inc(S["PGO"], 1)
                emit_casts(len(cast_jobs))
                for g in range(NGRP):
                    for sl in range(NSLOT):
                        je = g * NSLOT + sl

                        def wd_dma(hs):
                            js = sl * NHS + hs
                            slot = hs if hs <= 5 else hs - 6
                            gpsimd.wait_ge(S["CAST"], 16 * (3 * js + 3))
                            if hs <= 5:
                                if je >= 1:
                                    gpsimd.wait_ge(S["DNM"], je * 2 * NDCF)
                            else:
                                gpsimd.wait_ge(S["DNM"], (je * 2 + 1) * NDCF)
                            gpsimd.dma_start(out=wde[:, slot, :], in_=wdsc[js]
                                             ).then_inc(S["WLD"], 16)

                        for hs in range(NHS):
                            wd_dma(hs)

        @block.tensor
        def _(tensor):
            if do_m:
                tensor.wait_ge(S["SET"], 3)
                tensor.matmul(P1[:, 0:512], ones1_s[:], opd_row[0:1, 0:512],
                              start=True, stop=True)
                tensor.matmul(P1[:, 512:1024], ones1_s[:], opd_row[0:1, 512:1024],
                              start=True, stop=True).then_inc(S["SET"], 1)
                tensor.wait_ge(S["LD"], NLD)
                tensor.wait_ge(S["LDG"], 32)
                for k in range(NCH):
                    for ts_ in range(4):
                        it = 4 * k + ts_
                        tensor.wait_ge(S["M0X"], 16 * (k + 1))
                        if it >= 2:
                            tensor.wait_ge(S["M0E"], it - 1)
                        sv = PB[3 - (ts_ % 2)][:].bitcast(BF16)
                        for dc in range(NDCF):
                            ins = tensor.transpose(
                                sv[:, dc * 128:(dc + 1) * 128],
                                m0_xb[:, ts_, dc * 128:(dc + 1) * 128],
                                identb_s[:])
                        ins.then_inc(S["M0T"], 1)
                    tensor.wait_ge(S["M0E"], 4 * (k + 1))
                    if k == 0:
                        tensor.wait_ge(S["SET"], 5)
                    else:
                        tensor.wait_ge(S["M0V"], k)
                    for c in range(NDCF):
                        tensor.matmul(P0[0:80, 0:512], wxT_s[:, c, 0:80],
                                      m0_xf[:, c, :], start=(c == 0),
                                      stop=(c == NDCF - 1))
                    for c in range(NDCF):
                        ins = tensor.matmul(P1[0:16, 0:512], wxT_s[:, c, 80:96],
                                            m0_xf[:, c, :], start=(c == 0),
                                            stop=(c == NDCF - 1))
                    ins.then_inc(S["M0M"], 1)
                for k in range(NCH):
                    for ts_ in range(4):
                        it = kt(k, ts_)
                        tensor.wait_ge(S["XTB"], 16 * (k + 1))
                        if it >= 2:
                            tensor.wait_ge(S["UEV"], it - 1)
                        elif k == 0:
                            tensor.wait_ge(S["M0E"], 4 * NCH)
                        half = (ts_ % 2) * 1024
                        sv = P3[:].bitcast(BF16)
                        for dc in range(NDC):
                            ins = tensor.transpose(
                                sv[:, half + dc * 128:half + (dc + 1) * 128],
                                x_tmb[:, ts_, dc * 128:(dc + 1) * 128],
                                identb_s[:])
                        ins.then_inc(S["UTR"], 1)
                    def bc_build(h, pieces=range(8)):
                        kh = k * 2 + h
                        for p in pieces:
                            if k == 0 and h == 0 and p == 0:
                                tensor.wait_ge(S["M0V"], NCH)
                            if 8 * kh + p >= 2:
                                tensor.wait_ge(S["BCE"], 8 * kh + p - 1)
                            if k >= 1 and h == 0 and p <= 1:
                                tensor.wait_ge(S["ZST"], 4 * k)
                            src = bm_s if p < 4 else cm_s
                            pp = p % 4
                            tgt = P1 if p % 2 == 0 else P2
                            for h2 in range(2):
                                ng = h * NH + 2 * pp + h2
                                ins = tensor.matmul(
                                    tgt[:, h2 * 512:(h2 + 1) * 512],
                                    seln_s[:, ng * 128:(ng + 1) * 128],
                                    src[:, k * TC:(k + 1) * TC],
                                    start=True, stop=True)
                            ins.then_inc(S["BCM"], 1)

                    def delta_mm(dc):
                        i = kd(k, dc)
                        if i == 0:
                            tensor.wait_ge(S["M0V"], NCH)
                        if i >= 2:
                            tensor.wait_ge(S["SPS"], i - 1)
                        tensor.matmul(
                            P0[:, (dc % 2) * 512:(dc % 2 + 1) * 512],
                            wdT_s[:, dc * 128:(dc + 1) * 128],
                            draw_s[:, k * TC:(k + 1) * TC],
                            start=True, stop=True).then_inc(S["DMM"], 1)

                    bc_build(0, range(0, 4))
                    delta_mm(0)
                    delta_mm(1)
                    bc_build(0, range(4, 8))
                    for dc in range(2, NDC):
                        delta_mm(dc)
                    bc_build(1)
                    tensor.wait_ge(S["YDC"], 2 * NDC * (k + 1))
                    tensor.wait_ge(S["BCE"], 8 * (2 * k + 2))
                    for ts_ in range(4):
                        it = kt(k, ts_)
                        if it >= 2:
                            tensor.wait_ge(S["ZST"], it - 1)
                        tgt = P1 if ts_ % 2 == 0 else P2
                        for dc in range(NDC):
                            ins = tensor.transpose(
                                tgt[:, dc * 128:(dc + 1) * 128],
                                yac[:, dc, ts_ * 128:(ts_ + 1) * 128],
                                ident_s[:])
                        ins.then_inc(S["ZTR"], 1)
            if do_f:
                if fused:
                    tensor.wait_ge(S["PGO"], 1)
                tensor.wait_ge(S["LD"], NLD)
                for g in range(NGRP):
                    gof = g * NTIL
                    if g >= 1:
                        tensor.wait_ge(S["DAC"], g * NSLOT * 2 * NDCF)
                        tensor.wait_ge(S["OEV"], 16 * g)
                    for i in range(NTIL):
                        gi = gof + i
                        tensor.wait_ge(S["LNN"], gi + 1)
                        pa, pb_ = (P0, P1) if i % 2 == 0 else (P2, P3)
                        if gi >= 2:
                            tensor.wait_ge(S["X2E"], gi - 1)
                            tensor.wait_ge(S["X2F"], gi - 1)
                        for dcf in range(NDCF):
                            tgt = pa if dcf < 8 else pb_
                            ins = tensor.transpose(
                                tgt[:, (dcf % 8) * 128:(dcf % 8 + 1) * 128],
                                x2t[:, 0, dcf * 128:(dcf + 1) * 128],
                                ident_s[:])
                        ins.then_inc(S["X2T"], 1)
                    tensor.wait_ge(S["X2E"], gof + NTIL)
                    tensor.wait_ge(S["X2F"], gof + NTIL)
                    if not FFN_GATE:
                        gate_skip = True
                    for tt in (range(2) if FFN_GATE else []):
                        for c in range(NDCF):
                            ins = tensor.matmul(
                                P2[0:8, tt * 512:(tt + 1) * 512], gwT_s[:, c, :],
                                acc[:, c, tt * 512:(tt + 1) * 512],
                                start=(c == 0), stop=(c == NDCF - 1))
                    if FFN_GATE:
                        ins.then_inc(S["GLM"], 1)
                    for tt in (range(2) if FFN_GATE else []):
                        for c in range(NDCF):
                            ins = tensor.matmul(
                                P0[0:1, tt * 512:(tt + 1) * 512], sgw_s[:, c:c + 1],
                                acc[:, c, tt * 512:(tt + 1) * 512],
                                start=(c == 0), stop=(c == NDCF - 1))
                    ins.then_inc(S["SGM"], 1)
                    tensor.wait_ge(S["GLE"], 2 * g + 1)
                    for tt in range(NTIL):
                        ins = tensor.transpose(P3[:, tt * 8:(tt + 1) * 8],
                                               lg_fm[0:8, tt * 128:(tt + 1) * 128],
                                               ident_s[0:8, 0:8])
                    ins.then_inc(S["GTT"], 1)
                    tensor.wait_ge(S["CMB"], g + 1)
                    for tt in range(NTIL):
                        ins = tensor.transpose(P2[0:8, tt * 128:(tt + 1) * 128],
                                               carg[:, tt, :], ident_s[:])
                    ins.then_inc(S["GTT"], 1)
                    tensor.wait_ge(S["SGE"], 2 * g + 1)
                    tensor.wait_ge(S["GLE"], 2 * g + 2)
                    for h2 in range(2):
                        ins = tensor.matmul(
                            P3[:, h2 * 512:(h2 + 1) * 512], ones1b_s[:],
                            sg_row[0:1, h2 * 512:(h2 + 1) * 512],
                            start=True, stop=True)
                    ins.then_inc(S["SGM"], 1)
                    for sl in range(NSLOT):
                        je = g * NSLOT + sl
                        if sl < E:
                            tensor.wait_ge(S["CFME"], g + 1)
                            if je >= 1:
                                tensor.wait_ge(S["SIL"], je * NHS)
                            else:
                                tensor.wait_ge(S["SGE"], 2 * g + 1)
                            for h2 in range(2):
                                ins = tensor.matmul(
                                    P0[:, h2 * 512:(h2 + 1) * 512],
                                    sel8_s[:, sl * 128:(sl + 1) * 128],
                                    comb_fm[0:8, h2 * 512:(h2 + 1) * 512],
                                    start=True, stop=True)
                            ins.then_inc(S["CBC"], 1)
                        for hs in range(NHS):
                            j = je * NHS + hs
                            tensor.wait_ge(S["WLG"], 16 * (j + 1))
                            if j >= 1:
                                tensor.wait_ge(S["SIL"], j)
                            if hs == 0 and sl < E:
                                tensor.wait_ge(S["CBE"], g * E + sl + 1)
                            if hs == 0 and sl == 0:
                                tensor.wait_ge(S["SGE"], 2 * g + 2)
                            for tt in range(2):
                                for c in range(NDCF):
                                    ins = tensor.matmul(
                                        P0[:, tt * 512:(tt + 1) * 512],
                                        wgb[:, 0, c * 128:(c + 1) * 128],
                                        x2b[:, c, tt * 512:(tt + 1) * 512],
                                        start=(c == 0), stop=(c == NDCF - 1))
                            ins.then_inc(S["GUM"], 1)
                            tensor.wait_ge(S["WLU"], 16 * (j + 1))
                            if j >= 1:
                                tensor.wait_ge(S["HT1"], j)
                            for tt in range(2):
                                for c in range(NDCF):
                                    ins = tensor.matmul(
                                        P1[:, tt * 512:(tt + 1) * 512],
                                        wub[:, 0, c * 128:(c + 1) * 128],
                                        x2b[:, c, tt * 512:(tt + 1) * 512],
                                        start=(c == 0), stop=(c == NDCF - 1))
                            ins.then_inc(S["UUM"], 1)
                        tensor.wait_ge(S["WLD"], 16 * (je * NHS + 6))
                        tensor.wait_ge(S["HPD"], je * NHS + 6)
                        for pas in range(2):
                            if pas == 1:
                                tensor.wait_ge(S["WLD"], 16 * (je + 1) * NHS)
                                tensor.wait_ge(S["HPD"], (je + 1) * NHS)
                            hs_list = ([0, 1, 2, 3, 4, 5] if pas == 0
                                       else [6, 7, 8, 9, 10])
                            if sl == 0 and pas == 0:
                                tensor.wait_ge(S["CFME"], g + 1)
                                tensor.wait_ge(S["SGE"], 2 * (g + 1))
                            for dcf in range(NDCF):
                                jd = (je * 2 + pas) * NDCF + dcf
                                if jd >= 2:
                                    tensor.wait_ge(S["DAC"], jd - 1)
                                tgt = P2 if dcf % 2 == 0 else P3
                                for tt in range(2):
                                    for ii, hs in enumerate(hs_list):
                                        slot = hs if hs <= 5 else hs - 6
                                        ins = tensor.matmul(
                                            tgt[:, tt * 512:(tt + 1) * 512],
                                            wde[:, slot,
                                                dcf * 128:(dcf + 1) * 128],
                                            hpb[:, hs, tt * 512:(tt + 1) * 512],
                                            start=(ii == 0),
                                            stop=(ii == len(hs_list) - 1))
                                ins.then_inc(S["DNM"], 1)
                    tensor.wait_ge(S["DAC"], (g + 1) * NSLOT * 2 * NDCF)
                    for i in range(NTIL):
                        gi = gof + i
                        if gi >= 1:
                            tensor.wait_ge(S["OEV"], 2 * gi)
                        for dcf in range(NDCF):
                            tgt = P0 if dcf < 8 else P1
                            ins = tensor.transpose(
                                tgt[:, (dcf % 8) * 128:(dcf % 8 + 1) * 128],
                                acc[:, dcf, i * 128:(i + 1) * 128], ident_s[:])
                        ins.then_inc(S["OTR"], 1)

        @block.vector
        def _(vector):
            mix = [0]

            def vop(ins):
                mix[0] += 1
                ins.then_inc(S["MIX"], 1)

            def vwait():
                vector.wait_ge(S["MIX"], mix[0])

            vector.wait_ge(S["LD"], NLD)
            parc = [0]

            def pwait(ins, bit):
                if PARANOID & (1 << bit):
                    parc[0] += 1
                    ins.then_inc(S["PAR"], 1)
                    vector.wait_ge(S["PAR"], parc[0])
                return ins
            if do_m:
                vector.wait_ge(S["SET"], 1)
                vector.tensor_scalar(out=acol_s[:], in0=acol_s[:], scalar1=-1.0,
                                     scalar2=None, op0=OP.mult).then_inc(S["SET"], 1)
                vector.tensor_scalar(out=opd_row[:], in0=opd_row[:], scalar1=1.0,
                                     scalar2=None, op0=OP.add).then_inc(S["SET"], 1)
                for k in range(NCH):
                    for h in range(2):
                        kh = 2 * k + h
                        for dc in range(NDC):
                            i = kd(k, dc)
                            ih = kh * NDC + dc
                            vector.wait_ge(S["DBX"], ih + 1)
                            vector.wait_ge(S["DAS"], ih + 1)
                            for n in range(NH):
                                ng = h * NH + n
                                init = 0.0 if k == 0 else \
                                    state_s[:, dc, ng:ng + 1]
                                ins = pwait(vector.tensor_tensor_scan(
                                    out=hsc[:, n, :], data0=dA[:, n, :],
                                    data1=dbx[:, dc % 2, n * TC:(n + 1) * TC],
                                    initial=init,
                                    op0=OP.mult, op1=OP.add), 3)
                            ins.then_inc(S["SCN"], 1)
                            fine = (h == 0 and dc == 0)
                            vector.wait_ge(S["BCE"],
                                           8 * kh + (5 if fine else 8))
                            if k >= 1 and dc == 0 and h == 0:
                                vector.wait_ge(S["ZTR"], 4 * k)
                            if h == 0:
                                vector.tensor_mul(out=yac[:, dc, :],
                                                  in0=hsc[:, 0, :],
                                                  in1=crep[:, 0, :])
                                planes = list(range(1, NH))
                            else:
                                planes = list(range(NH))

                            def ym(j):
                                n = planes[j]
                                if fine and n % 2 == 0:
                                    vector.wait_ge(S["BCE"],
                                                   8 * kh + 5 + n // 2)
                                vector.tensor_mul(out=ytmp[:, j % 4, :],
                                                  in0=hsc[:, n, :],
                                                  in1=crep[:, n, :])
                            ym(0)
                            ym(1)
                            for j in range(len(planes)):
                                if j + 2 < len(planes):
                                    ym(j + 2)
                                if j == len(planes) - 1:
                                    vector.tensor_copy(
                                        out=state_s[:, dc,
                                                    h * NH:(h + 1) * NH],
                                        in_=hsc[:, :, TC - 1])
                                parc[0] += 1
                                vector.tensor_add(out=yac[:, dc, :],
                                                  in0=yac[:, dc, :],
                                                  in1=ytmp[:, j % 4, :]
                                                  ).then_inc(S["PAR"], 1)
                                vector.wait_ge(S["PAR"], parc[0])
                            vector.sem_inc(S["YDC"], 1)
                    for ts_ in range(4):
                        it = kt(k, ts_)
                        vector.wait_ge(S["ZTR"], it + 1)
                        vector.wait_ge(S["WTM"], it + 1)
                        src = P1 if ts_ % 2 == 0 else P2
                        vector.scalar_tensor_tensor(
                            out=x_tm[:, ts_, :], in0=src[:], scalar=1.0,
                            in1=wtm[:, ts_, :], op0=OP.mult,
                            op1=OP.add).then_inc(S["ZST"], 1)
            if do_f:
                if fused:
                    vector.wait_ge(S["PGO"], 1)
                inv_d = 1.0 / D

                def ln_micro():
                    vop(vector.tensor_scalar(out=stat[:, 2:3], in0=stat[:, 0:1],
                                             scalar1=inv_d, scalar2=None,
                                             op0=OP.mult))
                    vop(vector.tensor_scalar(out=stat[:, 3:4], in0=stat[:, 1:2],
                                             scalar1=inv_d, scalar2=None,
                                             op0=OP.mult))
                    vwait()
                    vop(vector.tensor_scalar(out=stat[:, 4:5], in0=stat[:, 2:3],
                                             scalar1=stat[:, 2:3], scalar2=None,
                                             op0=OP.mult))
                    vwait()
                    vop(vector.tensor_sub(out=stat[:, 4:5], in0=stat[:, 3:4],
                                          in1=stat[:, 4:5]))
                    vwait()
                    vop(vector.tensor_scalar(out=stat[:, 4:5], in0=stat[:, 4:5],
                                             scalar1=EPS, scalar2=None,
                                             op0=OP.add))
                    vwait()
                    sqrt_wait.append(mix[0])
                    vector.wait_ge(S["MICS"], len(sqrt_wait))
                    vop(vector.reciprocal(out=stat[:, 6:7], in_=stat[:, 5:6]))
                    vwait()

                for g in range(NGRP):
                    gof = g * NTIL
                    for i in range(NTIL):
                        gi = gof + i
                        vector.wait_ge(S["ZLD"], 16 * (gi + 1))
                        vop(vector.reduce_sum(out=stat[:, 0:1],
                                              in_=ztile[:, gi % 2, :],
                                              axis=AX.X))
                        vector.wait_ge(S["LNS"], gi + 1)
                        vwait()
                        ln_micro()
                        if gi >= 1:
                            vector.wait_ge(S["X2T"], gi)
                        vector.tensor_scalar(out=x2t[:, 0, :],
                                             in0=ztile[:, gi % 2, :],
                                             scalar1=stat[:, 2:3],
                                             scalar2=stat[:, 6:7],
                                             op0=OP.subtract,
                                             op1=OP.mult).then_inc(S["LNN"], 1)
                        vector.wait_ge(S["X2T"], gi + 1)
                        vector.wait_ge(S["X2E"], gi + 1)
                        pa, pb_ = (P0, P1) if i % 2 == 0 else (P2, P3)
                        vector.tensor_copy(out=acc[:, 0:8, i * 128:(i + 1) * 128],
                                           in_=pa[:].rearrange(
                                               "p (c h) -> p c h", c=8))
                        vector.tensor_copy(out=acc[:, 8:16, i * 128:(i + 1) * 128],
                                           in_=pb_[:].rearrange(
                                               "p (c h) -> p c h", c=8)
                                           ).then_inc(S["X2F"], 1)
                    vector.wait_ge(S["GLE"], 2 * g + 2)
                    for tt in range(NTIL):
                        lgv = lgT[:, tt, :]
                        vop(vector.reduce_max(out=stat[:, 8:9], in_=lgv,
                                              axis=AX.X))
                        vwait()
                        vop(vector.tensor_scalar(out=cmask[:, tt, :], in0=lgv,
                                                 scalar1=stat[:, 8:9],
                                                 scalar2=-1e30,
                                                 op0=OP.is_ge, op1=OP.mult))
                        vwait()
                        vop(vector.tensor_add(out=carg[:, tt, :], in0=lgv,
                                              in1=cmask[:, tt, :]))
                        vwait()
                        vop(vector.reduce_max(out=stat[:, 9:10],
                                              in_=carg[:, tt, :], axis=AX.X))
                        vwait()
                        vop(vector.tensor_scalar(out=cmask[:, tt, :], in0=lgv,
                                                 scalar1=stat[:, 9:10],
                                                 scalar2=None, op0=OP.is_ge))
                        vop(vector.tensor_tensor(out=stat[:, 10:11],
                                                 in0=stat[:, 8:9],
                                                 in1=stat[:, 9:10], op=OP.add))
                        vwait()
                        vop(vector.tensor_scalar(out=carg[:, tt, :], in0=lgv,
                                                 scalar1=2.0,
                                                 scalar2=stat[:, 10:11],
                                                 op0=OP.mult, op1=OP.subtract))
                        vwait()
                    vector.sem_inc(S["TOP"], 1)
                    vector.wait_ge(S["SIG"], g + 1)
                    vector.tensor_mul(out=carg[:], in0=carg[:],
                                      in1=cmask[:]).then_inc(S["CMB"], 1)
                    if g >= 1:
                        vector.wait_ge(S["DNM"], g * NSLOT * 2 * NDCF)
                    for sl in range(NSLOT):
                        je = g * NSLOT + sl
                        if sl < E:
                            vector.wait_ge(S["CBE"], g * E + sl + 1)
                        else:
                            vector.wait_ge(S["SGE"], 2 * g + 2)
                        for hs in range(NHS):
                            j = je * NHS + hs
                            vector.wait_ge(S["SIL"], j + 1)
                            vector.wait_ge(S["UUM"], j + 1)
                            vector.tensor_mul(out=hpt[:], in0=outt[:, 0:GTOK],
                                              in1=P1[:]).then_inc(S["HT1"], 1)
                            vector.wait_ge(S["HT1"], j + 1)
                            scl = sg_rep[:] if sl >= E else comb_e[:, sl % 2, :]
                            vector.tensor_mul(out=hpb[:, hs, :], in0=hpt[:],
                                              in1=scl).then_inc(S["HPD"], 1)
                            vector.wait_ge(S["HPD"], j + 1)
                        for pas in range(2):
                            for dcf in range(NDCF):
                                jd = (je * 2 + pas) * NDCF + dcf
                                vector.wait_ge(S["DNM"], jd + 1)
                                src = P2 if dcf % 2 == 0 else P3
                                vector.tensor_add(out=acc[:, dcf, :],
                                                  in0=acc[:, dcf, :],
                                                  in1=src[:]).then_inc(S["DAC"], 1)
                    vector.wait_ge(S["ZLD"], 16 * NTIL * (g + 1))
                    for i in range(NTIL):
                        gi = gof + i
                        vector.wait_ge(S["LN2S"], gi + 1)
                        vop(vector.reduce_sum(out=stat[:, 0:1], in_=outt[:],
                                              axis=AX.X))
                        vwait()
                        ln_micro()
                        if gi >= 2:
                            vector.wait_ge(S["ODM"], 16 * (gi - 1))
                        vector.tensor_scalar(out=ztile[:, gi % 2, :], in0=outt[:],
                                             scalar1=stat[:, 2:3],
                                             scalar2=stat[:, 6:7],
                                             op0=OP.subtract,
                                             op1=OP.mult).then_inc(S["LN2N"], 1)

        @block.scalar
        def _(scalar):
            scalar.wait_ge(S["LD"], NLD)
            if do_m:
                scalar.activation(out=acol_s[:], in_=acol_s[:],
                                  func=AF.Exp).then_inc(S["SET"], 1)
                scalar.wait_ge(S["SET"], 4)
                scalar.copy(out=opd_rep[:], in_=P1[:]).then_inc(S["SET"], 1)
                for k in range(NCH):
                    for ts_ in range(4):
                        it = 4 * k + ts_
                        scalar.wait_ge(S["M0T"], it + 1)
                        sv = PB[3 - (ts_ % 2)][:].bitcast(BF16)
                        scalar.copy(
                            out=m0_xf[:, :, ts_ * 128:(ts_ + 1) * 128],
                            in_=sv.rearrange("p (c h) -> p c h", c=NDCF),
                        ).then_inc(S["M0E"], 1)
                    scalar.wait_ge(S["M0M"], k + 1)
                    scalar.copy(out=draw_s[:, k * TC:(k + 1) * TC],
                                in_=P0[0:64, 0:512])
                    scalar.copy(out=bm_s[:, k * TC:(k + 1) * TC],
                                in_=P0[64:80, 0:512])
                    scalar.copy(out=cm_s[:, k * TC:(k + 1) * TC],
                                in_=P1[0:16, 0:512]).then_inc(S["M0V"], 1)
                for k in range(NCH):
                    for ts_ in range(4):
                        it = kt(k, ts_)
                        scalar.wait_ge(S["UTR"], it + 1)
                        half = (ts_ % 2) * 1024
                        scalar.copy(
                            out=xfmh[:, :, ts_ * 128:(ts_ + 1) * 128],
                            in_=P3[:].bitcast(BF16)[:, half:half + 1024]
                            .rearrange("p (c h) -> p c h", c=NDC),
                        ).then_inc(S["UEV"], 1)

                    def bc_evacs(h, pieces=range(8)):
                        kh = 2 * k + h
                        for p in pieces:
                            scalar.wait_ge(S["BCM"], 8 * kh + p + 1)
                            if kh >= 1:
                                scalar.wait_ge(S["SCN"] if p < 4 else S["YDC"],
                                               NDC * kh)
                            src = P1 if p % 2 == 0 else P2
                            dstt = brep if p < 4 else crep
                            pp = p % 4
                            scalar.copy(
                                out=dstt[:, 2 * pp:2 * pp + 2, :],
                                in_=src[:].rearrange("p (a t) -> p a t", a=2),
                            ).then_inc(S["BCE"], 1)

                    def exp_dc(h, dc):
                        i = kd(k, dc)
                        ih = (2 * k + h) * NDC + dc
                        if h == 0:
                            scalar.wait_ge(S["SPP"], i + 1)
                        if ih >= 1:
                            scalar.wait_ge(S["SCN"], ih)
                        for n in range(NH):
                            ng = h * NH + n
                            ins = scalar.activation(
                                out=dA[:, n, :], in_=delta_s[:, dc, :],
                                func=AF.Exp,
                                scale=acol_s[:, dc * N + ng:dc * N + ng + 1])
                        ins.then_inc(S["DAS"], 1)

                    bc_evacs(0, range(0, 4))
                    for dc in range(NDC):
                        i = kd(k, dc)
                        scalar.wait_ge(S["DMM"], i + 1)
                        if i >= 2:
                            scalar.wait_ge(S["SPP"], i - 1)
                        ps = P0[:, (dc % 2) * 512:(dc % 2 + 1) * 512]
                        bb = bdt_s[:, dc:dc + 1]
                        scalar.activation(out=sp1[:, i % 2, :], in_=ps,
                                          func=AF.Abs, bias=bb)
                        scalar.activation(out=sp2[:, i % 2, :], in_=ps,
                                          func=AF.Relu,
                                          bias=bb).then_inc(S["SPS"], 1)
                        scalar.activation(out=sp1[:, i % 2, :],
                                          in_=sp1[:, i % 2, :],
                                          func=AF.Exp, scale=-1.0)
                        scalar.activation(out=sp1[:, i % 2, :],
                                          in_=sp1[:, i % 2, :],
                                          func=AF.Ln,
                                          bias=1.0).then_inc(S["SPA"], 1)
                        exp_dc(0, dc)
                        if dc == 0:
                            bc_evacs(0, [4, 5])
                        elif dc == 1:
                            bc_evacs(0, [6, 7])
                    bc_evacs(1)
                    for dc in range(NDC):
                        exp_dc(1, dc)
            if do_f:
                if fused:
                    scalar.wait_ge(S["PGO"], 1)
                    rzp = scalar.alloc_register("rzp")
                    scalar.load(rzp, cinfo_s[0:1, 0:1])
                    zdhv = scalar.snap(rzp, min_val=0, max_val=1)
                    zsh_ld = ap["zsh"].rearrange(
                        "(u v w p) d -> v u w p d", p=128, w=4, v=2)[zdhv]
                q = 0
                for g in range(NGRP):
                    gof = g * NTIL
                    if g >= 1:
                        scalar.wait_ge(S["GUM"], g * NSLOT * NHS)
                        scalar.wait_ge(S["UUM"], g * NSLOT * NHS)
                    for i in range(NTIL):
                        gi = gof + i
                        if gi >= 2:
                            scalar.wait_ge(S["LNN"], gi - 1)
                        if g >= 1 and i < 2:
                            scalar.wait_ge(S["ODM"], 16 * (NTIL * g - 1 + i))
                        if mode == "ffn":
                            scalar.dma_start(
                                out=ztile[:, gi % 2, :],
                                in_=ap["zin"][gi * 128:(gi + 1) * 128, :],
                            ).then_inc(S["ZLD"], 16)
                        else:
                            scalar.dma_start(
                                out=ztile[:, gi % 2, :],
                                in_=zsh_ld[2 * g + i // 4, i % 4],
                            ).then_inc(S["ZLD"], 16)
                        if g >= 1 and i == 0:
                            scalar.wait_ge(S["LN2N"], NTIL * g)
                        scalar.wait_ge(S["ZLD"], 16 * (gi + 1))
                        scalar.activation(out=outt[:], in_=ztile[:, gi % 2, :],
                                          func=AF.Square,
                                          accum_out=stat[:, 1:2]).then_inc(S["LNS"], 1)
                        scalar.wait_ge(S["MIX"], sqrt_wait[q])
                        scalar.activation(out=stat[:, 5:6], in_=stat[:, 4:5],
                                          func=AF.Sqrt).then_inc(S["MICS"], 1)
                        q += 1
                        scalar.wait_ge(S["X2T"], gi + 1)
                        pa, pb_ = (P0, P1) if i % 2 == 0 else (P2, P3)
                        scalar.copy(out=x2b[:, 0:8, i * 128:(i + 1) * 128],
                                    in_=pa[:].rearrange("p (c h) -> p c h", c=8))
                        scalar.copy(out=x2b[:, 8:16, i * 128:(i + 1) * 128],
                                    in_=pb_[:].rearrange("p (c h) -> p c h", c=8)
                                    ).then_inc(S["X2E"], 1)
                    scalar.wait_ge(S["GLM"], g + 1)
                    scalar.copy(out=lg_fm[:], in_=P2[0:8, :]).then_inc(S["GLE"], 1)
                    scalar.wait_ge(S["SGM"], 2 * g + 1)
                    scalar.activation(out=sg_row[:], in_=P0[0:1, :],
                                      func=AF.Sigmoid).then_inc(S["SGE"], 1)
                    scalar.wait_ge(S["GTT"], 2 * g + 1)
                    scalar.copy(out=lgT[:],
                                in_=P3[:, 0:NTIL * 8]
                                .rearrange("p (t e) -> p t e", t=NTIL)
                                ).then_inc(S["GLE"], 1)
                    scalar.wait_ge(S["TOP"], g + 1)
                    scalar.activation(out=carg[:], in_=carg[:],
                                      func=AF.Sigmoid).then_inc(S["SIG"], 1)
                    scalar.wait_ge(S["GTT"], 2 * g + 2)
                    scalar.copy(out=comb_fm[:], in_=P2[0:8, :]).then_inc(S["CFME"], 1)
                    scalar.wait_ge(S["SGM"], 2 * g + 2)
                    if g >= 1:
                        scalar.wait_ge(S["HPD"], g * NSLOT * NHS)
                    scalar.copy(out=sg_rep[:], in_=P3[:]).then_inc(S["SGE"], 1)
                    for sl in range(NSLOT):
                        je = g * NSLOT + sl
                        if sl < E:
                            scalar.wait_ge(S["CBC"], g * E + sl + 1)
                            if g * E + sl >= 2:
                                scalar.wait_ge(S["HPD"], (je - 1) * NHS)
                            scalar.copy(out=comb_e[:, sl % 2, :],
                                        in_=P0[:]).then_inc(S["CBE"], 1)
                        for hs in range(NHS):
                            j = je * NHS + hs
                            scalar.wait_ge(S["GUM"], j + 1)
                            if j >= 1:
                                scalar.wait_ge(S["HT1"], j)
                            scalar.activation(out=outt[:, 0:GTOK], in_=P0[:],
                                              func=AF.Silu).then_inc(S["SIL"], 1)
                    for i in range(NTIL):
                        gi = gof + i
                        scalar.wait_ge(S["OTR"], gi + 1)
                        if i == 0:
                            scalar.wait_ge(S["X2T"], NTIL * (g + 1))
                        if gi >= 1:
                            scalar.wait_ge(S["LN2N"], gi)
                        scalar.copy(out=outt[:, 0:1024], in_=P0[:]).then_inc(S["OEV"], 1)
                        scalar.copy(out=outt[:, 1024:2048],
                                    in_=P1[:]).then_inc(S["OEV"], 1)
                        scalar.activation(out=x2t[:, 0, :], in_=outt[:],
                                          func=AF.Square,
                                          accum_out=stat[:, 1:2]).then_inc(S["LN2S"], 1)
                        scalar.wait_ge(S["MIX"], sqrt_wait[q])
                        scalar.activation(out=stat[:, 5:6], in_=stat[:, 4:5],
                                          func=AF.Sqrt).then_inc(S["MICS"], 1)
                        q += 1

    return nc



def _sel_matrix(nrows, nsel, width=128):
    m = np.zeros((nrows, nsel * width), ml_dtypes.bfloat16)
    for n in range(nsel):
        if n < nrows:
            m[n, n * width:(n + 1) * width] = 1.0
    return m


def _sel_n():
    m = np.zeros((16, N * 128), ml_dtypes.bfloat16)
    for ng in range(N):
        m[ng, ng * 128:(ng + 1) * 128] = 1.0
    return m


def _sel_8():
    m = np.zeros((8, E * 128), ml_dtypes.bfloat16)
    for e in range(E):
        m[e, e * 128:(e + 1) * 128] = 1.0
    return m


def _prep_core_inputs(c, full, nonce, mode):
    b_, dh = c // 2, c % 2
    d0 = dh * DH
    f32 = np.float32
    out = {
        "ident": np.eye(128, dtype=f32),
        "identb": np.eye(128, dtype=ml_dtypes.bfloat16),
        "ones1": np.ones((1, 128), f32),
        "ones1b": np.ones((1, 128), ml_dtypes.bfloat16),
        "cinfo": np.array([[dh, 1 - dh, nonce, 0, 0, 0, 0, 0]], np.int32),
    }
    if mode in ("mamba", "fused"):
        xb = np.ascontiguousarray(np.asarray(full["L0_attn_x"][b_], f32))
        out["x"] = xb
        out["xh"] = np.ascontiguousarray(xb[:, d0:d0 + DH])
        out["wxT"] = np.ascontiguousarray(np.asarray(full["L0_attn_W_xproj"], f32).T)
        out["wdT"] = np.ascontiguousarray(
            np.asarray(full["L0_attn_W_dt"], f32)[d0:d0 + DH].T)
        out["bdt"] = np.ascontiguousarray(
            np.asarray(full["L0_attn_b_dt"], f32)[d0:d0 + DH].reshape(NDC, 128).T)
        out["alog"] = np.ascontiguousarray(
            np.asarray(full["L0_attn_A_log"], f32)[d0:d0 + DH]
            .reshape(NDC, 128, N).transpose(1, 0, 2).reshape(128, NDC * N))
        out["dprm"] = np.ascontiguousarray(
            np.asarray(full["L0_attn_D_param"], f32).reshape(1, D)[:, d0:d0 + DH])
        out["seln"] = _sel_n()
    if mode in ("ffn", "fused"):
        out["gwT"] = np.ascontiguousarray(np.asarray(full["L0_ffn_gate_w"], f32).T)
        out["sgw"] = np.ascontiguousarray(
            np.asarray(full["L0_ffn_shgate_w"], f32).reshape(NDCF, 128).T)
        out["sel8"] = _sel_8()
        out["wg"] = np.ascontiguousarray(np.asarray(full["L0_ffn_w_gate"], f32))
        out["wu"] = np.ascontiguousarray(np.asarray(full["L0_ffn_w_up"], f32))
        out["wd"] = np.ascontiguousarray(np.asarray(full["L0_ffn_w_down"], f32))
        out["swg"] = np.ascontiguousarray(np.asarray(full["L0_ffn_shw_gate"], f32))
        out["swu"] = np.ascontiguousarray(np.asarray(full["L0_ffn_shw_up"], f32))
        out["swd"] = np.ascontiguousarray(np.asarray(full["L0_ffn_shw_down"], f32))
    if mode == "ffn":
        pass
    return out


def _own_rows(dh):
    return np.concatenate([np.arange(ch * TC, (ch + 1) * TC)
                           for ch in (dh, dh + 2, dh + 4, dh + 6)])


def _scatter_out(res, key):
    out = np.empty((B, L, D), np.float32)
    for c in range(8):
        b_, dh = c // 2, c % 2
        oy = np.asarray(res.results[c][key]).reshape(4, TC, D)
        for j, ch in enumerate((dh, dh + 2, dh + 4, dh + 6)):
            out[b_, ch * TC:(ch + 1) * TC] = oy[j]
    return out


KERNEL_MODE = "fused"


def kernel(**inputs):
    if KERNEL_MODE == "fused":
        nonce = int(np.random.default_rng().integers(1, 2**31 - 1))
        nc = build_program("fused")
        in_maps = [_prep_core_inputs(c, inputs, nonce, "fused") for c in range(8)]
        res = run_bass_kernel_spmd(nc, in_maps, core_ids=list(range(8)))
        return _scatter_out(res, "oy")
    nc_m = build_program("mamba")
    in_maps = [_prep_core_inputs(c, inputs, 0, "mamba") for c in range(8)]
    res_m = run_bass_kernel_spmd(nc_m, in_maps, core_ids=list(range(8)))
    z = np.empty((B, L, D), np.float32)
    for c in range(8):
        b_, dh = c // 2, c % 2
        z[b_, :, dh * DH:(dh + 1) * DH] = np.asarray(res_m.results[c]["zout"])
    nc_f = build_program("ffn")
    in_maps = []
    for c in range(8):
        b_, dh = c // 2, c % 2
        m = _prep_core_inputs(c, inputs, 0, "ffn")
        m["zin"] = np.ascontiguousarray(z[b_][_own_rows(dh)])
        in_maps.append(m)
    res_f = run_bass_kernel_spmd(nc_f, in_maps, core_ids=list(range(8)))
    return _scatter_out(res_f, "oy")


# version 08 (final)
# speedup vs baseline: 1.1221x; 1.0599x over previous
import numpy as np
import ml_dtypes
import concourse.bass as bass
import concourse.mybir as mybir
from concourse.bass_utils import run_bass_kernel_spmd

F32 = mybir.dt.float32
BF16 = mybir.dt.bfloat16
I32 = mybir.dt.int32
AF = mybir.ActivationFunctionType
OP = mybir.AluOpType
AX = mybir.AxisListType

B, L, D, N, R = 4, 4096, 2048, 16, 64
DH = D // 2
TC = 512
NCH = L // TC
NDC = DH // 128
NDCF = D // 128
NH = N // 2
E, H, HS = 8, 1408, 2816
NHS = H // 128
WDS = 6
NTOK = L // 2
NGRP = 2
GTOK = NTOK // NGRP
NTIL = GTOK // 128
EPS = 1e-5
NSLOT = 10
ABASE = 17472
PARANOID = 0
FFN_GATE = True


def build_program(mode):
    assert mode in ("mamba", "ffn", "fused")
    do_m = mode in ("mamba", "fused")
    do_f = mode in ("ffn", "fused")
    fused = mode == "fused"

    nc = bass.Bass("TRN2", target_bir_lowering=False, debug=False)
    ap = {}

    def din(name, shape, dt=F32):
        ap[name] = nc.dram_tensor(name, shape, dt, kind="ExternalInput").ap()

    def dout(name, shape, dt=F32):
        ap[name] = nc.dram_tensor(name, shape, dt, kind="ExternalOutput").ap()

    din("ident", [128, 128]); din("identb", [128, 128], BF16)
    din("ones1", [1, 128]); din("ones1b", [1, 128], BF16)
    din("cinfo", [1, 8], I32)
    if do_m:
        din("x", [L, D]); din("xh", [L, DH])
        din("wxT", [D, 96]); din("wdT", [R, DH])
        din("bdt", [128, NDC]); din("alog", [128, NDC * N]); din("dprm", [1, DH])
        din("seln", [16, N * 128], BF16)
    if do_f:
        din("gwT", [D, E]); din("sgw", [128, NDCF])
        din("sel8", [8, E * 128], BF16)
        din("wg", [E, D, H]); din("wu", [E, D, H]); din("wd", [E, H, D])
        din("swg", [D, HS]); din("swu", [D, HS]); din("swd", [HS, D])
        dout("oy", [NTOK, D])
        wgsc = nc.dram_tensor("wgsc", [NSLOT * NHS, 128, NDCF * 128], BF16,
                              kind="Internal").ap()
        wusc = nc.dram_tensor("wusc", [NSLOT * NHS, 128, NDCF * 128], BF16,
                              kind="Internal").ap()
        wdsc = nc.dram_tensor("wdsc", [NSLOT * NHS, 128, D], BF16,
                              kind="Internal").ap()
    if mode == "mamba":
        dout("zout", [L, DH])
    elif mode == "ffn":
        din("zin", [NTOK, D])
    else:
        ap["zsh"] = nc.dram_tensor("zsh", [L, D], F32, addr_space="Shared").ap()
        ap["flg"] = nc.dram_tensor("flg", [2, 16], I32, addr_space="Shared").ap()

    pos = [ABASE]
    sb_off = {}

    def sb(name, shape, dtype, at=None):
        free = int(np.prod(shape[1:]))
        nbytes = free * (2 if dtype == BF16 else 4)
        if at is None:
            at = pos[0]
            pos[0] += (nbytes + 63) & ~63
        else:
            pos[0] = max(pos[0], at + ((nbytes + 63) & ~63))
        sb_off[name] = at
        return nc.alloc_sbuf_tensor_at(name, [shape[0]] + list(shape[1:]),
                                       dtype, offset=at)

    ident_s = sb("ident_s", [128, 128], F32)
    identb_s = sb("identb_s", [128, 128], BF16)
    ones1_s = sb("ones1_s", [1, 128], F32)
    ones1b_s = sb("ones1b_s", [1, 128], BF16)
    cinfo_s = sb("cinfo_s", [1, 8], I32)
    pollt = sb("pollt", [1, 4], I32)
    if do_f:
        gwT_s = sb("gwT_s", [128, NDCF, E], F32)
        sgw_s = sb("sgw_s", [128, NDCF], F32)
        sel8_s = sb("sel8_s", [8, E * 128], BF16)
    common_end = pos[0]

    if do_m:
        pos[0] = common_end
        wxT_s = sb("wxT_s", [128, NDCF, 96], BF16)
        wdT_s = sb("wdT_s", [64, DH], BF16)
        bdt_s = sb("bdt_s", [128, NDC], F32)
        acol_s = sb("acol_s", [128, NDC * N], F32)
        opd_row = sb("opd_row", [1, DH], F32)
        opd_rep = sb("opd_rep", [128, DH], F32)
        seln_s = sb("seln_s", [16, N * 128], BF16)
        draw_s = sb("draw_s", [64, L], BF16)
        bm_s = sb("bm_s", [16, L], BF16)
        cm_s = sb("cm_s", [16, L], BF16)
        state_s = sb("state_s", [128, NDC, N], F32)
        x_tm = sb("x_tm", [128, 4, DH], F32)
        x_tmb = sb("x_tmb", [128, 4, DH], BF16)
        xfmh = sb("xfmh", [128, NDC, TC], BF16)
        delta_s = sb("delta_s", [128, NDC, TC], F32)
        u_all = sb("u_all", [128, NDC, TC], BF16)
        dA = sb("dA", [128, NH, TC], F32)
        dbx = sb("dbx", [128, 2, NH * TC], BF16)
        hsc = sb("hsc", [128, NH, TC], BF16)
        yac = sb("yac", [128, NDC, TC], F32)
        wtm = sb("wtm", [128, 4, DH], F32)
        brep = sb("brep", [128, NH, TC], BF16)
        crep = sb("crep", [128, NH, TC], BF16)
        ytmp = sb("ytmp", [128, 4, TC], BF16)
        sp1 = sb("sp1", [128, 2, TC], F32)
        sp2 = sb("sp2", [128, 2, TC], F32)
        m0_xb = sb("m0_xb", [128, 4, D], BF16, at=sb_off["dA"])
        m0_xf = sb("m0_xf", [128, NDCF, TC], BF16, at=sb_off["yac"])
        m_end = pos[0]
    else:
        m_end = common_end

    if do_f:
        pos[0] = common_end
        acc = sb("acc", [128, NDCF, GTOK], F32)
        x2b = sb("x2b", [128, NDCF, GTOK], BF16)
        hpb = sb("hpb", [128, NHS, GTOK], BF16)
        wde = sb("wde", [128, 6, D], BF16)
        wgb = sb("wgb", [128, 1, NDCF * 128], BF16)
        wub = sb("wub", [128, 1, NDCF * 128], BF16)
        comb_e = sb("comb_e", [128, 2, GTOK], BF16)
        sg_rep = sb("sg_rep", [128, GTOK], BF16)
        ztile = sb("ztile", [128, 2, D], F32)
        x2t = sb("x2t", [128, 1, D], F32)
        outt = sb("outt", [128, D], F32)
        hpt = sb("hpt", [128, GTOK], BF16)
        stat = sb("stat", [128, 16], F32)
        lg_fm = sb("lg_fm", [8, GTOK], F32)
        lgT = sb("lgT", [128, NTIL, E], F32)
        carg = sb("carg", [128, NTIL, E], F32)
        cmask = sb("cmask", [128, NTIL, E], F32)
        comb_fm = sb("comb_fm", [8, GTOK], BF16)
        sg_row = sb("sg_row", [1, GTOK], BF16)
        f_end = pos[0]
    else:
        f_end = common_end

    assert max(m_end, f_end) <= 229376, (m_end, f_end)

    P0 = nc.alloc_psum_tensor("P0", [128, 1024], F32)
    P1 = nc.alloc_psum_tensor("P1", [128, 1024], F32)
    P2 = nc.alloc_psum_tensor("P2", [128, 1024], F32)
    P3 = nc.alloc_psum_tensor("P3", [128, 1024], F32)
    PB = [P0, P1, P2, P3]

    S = {}
    for nm in ["LD", "SET",
               "M0X", "M0T", "M0E", "M0M", "M0V",
               "XTF", "XTB", "UTR", "UEV", "DMM", "SPS", "SPA", "SPP",
               "BCM", "BCE", "TSA", "DAS", "SCN", "YDC", "ZTR", "WTM", "ZST",
               "ZDM", "FLG", "PGO", "PRT",
               "ZLD", "LNS", "LNN", "X2T", "X2E", "X2F",
               "GLM", "GLE", "GTT", "TOP", "SIG", "CMB",
               "CBC", "CBE", "SGM", "SGE",
               "WLG", "WLU", "WLD", "GUM", "UUM", "SIL", "HT1", "HPD",
               "DNM", "DAC", "OTR", "OEV", "LN2S", "LN2N", "ODM",
               "MIX", "MICS", "CFME", "LDG", "PAR", "DBX", "CAST"]:
        S[nm] = nc.alloc_semaphore(nm)

    setup = [("ident", ident_s[:], None), ("identb", identb_s[:], None),
             ("ones1", ones1_s[:], None), ("ones1b", ones1b_s[:], None),
             ("cinfo", cinfo_s[:], None)]
    if do_m:
        setup += [("bdt", bdt_s[:], None),
                  ("alog", acol_s[:], None), ("dprm", opd_row[:], None),
                  ("seln", seln_s[:], None)]
    if do_f:
        setup += [
            ("gwT", gwT_s[:], ap["gwT"].rearrange("(c p) e -> p c e", p=128)),
            ("sgw", sgw_s[:], None),
            ("sel8", sel8_s[:], None),
        ]
    NLD = len(setup) * 16

    cast_jobs = []
    if do_f:
        for sl in range(NSLOT):
            shared = sl >= E
            for hs in range(NHS):
                hh = hs if not shared else (sl - E) * NHS + hs
                js = sl * NHS + hs
                if shared:
                    srcg = ap["swg"][:, hh * 128:(hh + 1) * 128]
                    srcu = ap["swu"][:, hh * 128:(hh + 1) * 128]
                    srcd = ap["swd"][hh * 128:(hh + 1) * 128, :]
                else:
                    srcg = ap["wg"][sl, :, hh * 128:(hh + 1) * 128]
                    srcu = ap["wu"][sl, :, hh * 128:(hh + 1) * 128]
                    srcd = ap["wd"][sl, hh * 128:(hh + 1) * 128, :]
                cast_jobs.append(
                    (wgsc[js].rearrange("p (c h) -> p c h", c=NDCF),
                     srcg.rearrange("(c p) h -> p c h", p=128)))
                cast_jobs.append(
                    (wusc[js].rearrange("p (c h) -> p c h", c=NDCF),
                     srcu.rearrange("(c p) h -> p c h", p=128)))
                cast_jobs.append((wdsc[js], srcd))

    def kd(k, dc):
        return k * NDC + dc

    def kt(k, ts_):
        return k * 4 + ts_

    sqrt_wait = []

    with nc.Block() as block:

        @block.sync
        def _(sync):
            for name, dst, src in setup:
                sync.dma_start(out=dst, in_=(src if src is not None else ap[name])
                               ).then_inc(S["LD"], 16)
            if fused:
                rpar = sync.alloc_register("rpar")
                sync.wait_ge(S["LD"], 16 * 5)
                sync.load(rpar, cinfo_s[0:1, 0:1])
                dhv = sync.snap(rpar, min_val=0, max_val=1)
                zsh_st = ap["zsh"].rearrange(
                    "(c s p) (h d) -> h c p s d", p=128, s=4, h=2)[dhv]

            def zstore(k):
                src = x_tm[:]
                if mode == "mamba":
                    dst = ap["zout"][k * TC:(k + 1) * TC, :] \
                        .rearrange("(s p) d -> p s d", p=128)
                else:
                    dst = zsh_st[k]
                sync.dma_start(out=dst, in_=src).then_inc(S["ZDM"], 16)

            if do_m:
                for k in range(NCH):
                    if k >= 1:
                        sync.wait_ge(S["ZST"], 4 * k)
                        zstore(k - 1)
                        sync.wait_ge(S["ZDM"], 16 * k)
                    sync.dma_start(
                        out=x_tm[:],
                        in_=ap["xh"][k * TC:(k + 1) * TC, :]
                        .rearrange("(s p) d -> p s d", p=128),
                    ).then_inc(S["XTF"], 16)
                sync.wait_ge(S["ZST"], 4 * NCH)
                zstore(NCH - 1)
            if fused:
                sync.wait_ge(S["ZDM"], 16 * NCH)
                sync.dma_start(out=ap["flg"][dhv, 0:1],
                               in_=cinfo_s[0:1, 2:3]).then_inc(S["FLG"], 16)
            if do_f:
                def wgu_loads(j0, j1):
                    for j in range(j0, j1):
                        je, hs = divmod(j, NHS)
                        sl = je % NSLOT
                        js = sl * NHS + hs
                        sync.wait_ge(S["CAST"], 16 * (3 * js + 1))
                        if j >= 1:
                            sync.wait_ge(S["GUM"], j)
                        sync.dma_start(out=wgb[:, 0, :],
                                       in_=wgsc[js]).then_inc(S["WLG"], 16)
                        sync.wait_ge(S["CAST"], 16 * (3 * js + 2))
                        if j >= 1:
                            sync.wait_ge(S["UUM"], j)
                        sync.dma_start(out=wub[:, 0, :],
                                       in_=wusc[js]).then_inc(S["WLU"], 16)

                def ostores(g):
                    for i in range(NTIL):
                        gi = g * NTIL + i
                        sync.wait_ge(S["LN2N"], gi + 1)
                        sync.dma_start(out=ap["oy"][gi * 128:(gi + 1) * 128, :],
                                       in_=ztile[:, gi % 2, :]
                                       ).then_inc(S["ODM"], 16)

                wgu_loads(0, NSLOT * NHS + 1)
                ostores(0)
                wgu_loads(NSLOT * NHS + 1, NGRP * NSLOT * NHS)
                ostores(1)
                sync.wait_ge(S["ODM"], 16 * NGRP * NTIL)
            elif mode == "mamba":
                sync.wait_ge(S["ZDM"], 16 * NCH)

        @block.gpsimd
        def _(gpsimd):
            cast_pos = [0]

            def emit_casts(n):
                for _ in range(n):
                    if cast_pos[0] >= len(cast_jobs):
                        return
                    d, s = cast_jobs[cast_pos[0]]
                    gpsimd.dma_start(out=d, in_=s).then_inc(S["CAST"], 16)
                    cast_pos[0] += 1
            if do_m:
                gpsimd.dma_start(
                    out=wxT_s[:],
                    in_=ap["wxT"].rearrange("(c p) e -> p c e", p=128),
                ).then_inc(S["LDG"], 16)
                gpsimd.dma_start(out=wdT_s[:], in_=ap["wdT"]).then_inc(S["LDG"], 16)
                for k in range(NCH):
                    if k >= 1:
                        gpsimd.wait_ge(S["M0T"], 4 * k)
                    gpsimd.dma_start(
                        out=m0_xb[:],
                        in_=ap["x"][k * TC:(k + 1) * TC, :]
                        .rearrange("(s p) d -> p s d", p=128),
                    ).then_inc(S["M0X"], 16)
                for k in range(NCH):
                    if k >= 1:
                        gpsimd.wait_ge(S["UTR"], 4 * k)
                    gpsimd.dma_start(
                        out=x_tmb[:],
                        in_=ap["xh"][k * TC:(k + 1) * TC, :]
                        .rearrange("(s p) d -> p s d", p=128),
                    ).then_inc(S["XTB"], 16)
                    def dbx_dc(h, dc):
                        kh = 2 * k + h
                        ih = kh * NDC + dc
                        gpsimd.wait_ge(S["BCE"], 8 * kh + 4)
                        if ih >= 2:
                            gpsimd.wait_ge(S["SCN"], ih - 1)
                        for n in range(NH):
                            ins = gpsimd.tensor_mul(
                                out=dbx[:, dc % 2, n * TC:(n + 1) * TC],
                                in0=u_all[:, dc, :], in1=brep[:, n, :])
                        ins.then_inc(S["DBX"], 1)
                        emit_casts(3)

                    for dc in range(NDC):
                        i = kd(k, dc)
                        gpsimd.wait_ge(S["SPA"], i + 1)
                        gpsimd.tensor_add(out=delta_s[:, dc, :],
                                          in0=sp1[:, i % 2, :],
                                          in1=sp2[:, i % 2, :]).then_inc(S["SPP"], 1)
                        gpsimd.wait_ge(S["UEV"], 4 * (k + 1))
                        gpsimd.tensor_mul(out=u_all[:, dc, :],
                                          in0=delta_s[:, dc, :],
                                          in1=xfmh[:, dc, :])
                        dbx_dc(0, dc)
                    for dc in range(NDC):
                        dbx_dc(1, dc)
                    gpsimd.wait_ge(S["SET"], 5)
                    gpsimd.wait_ge(S["XTF"], 16 * (k + 1))
                    for ts_ in range(4):
                        if k >= 1:
                            gpsimd.wait_ge(S["ZST"], kt(k - 1, ts_) + 1)
                        gpsimd.tensor_mul(out=wtm[:, ts_, :], in0=x_tm[:, ts_, :],
                                          in1=opd_rep[:]).then_inc(S["WTM"], 1)
            if do_f:
                if fused:
                    gpsimd.wait_ge(S["FLG"], 16)
                    rpn = gpsimd.alloc_register("rpn")
                    rnon = gpsimd.alloc_register("rnon")
                    rcnt = gpsimd.alloc_register("rcnt")
                    rval = gpsimd.alloc_register("rval")
                    gpsimd.load(rpn, cinfo_s[0:1, 1:2])
                    gpsimd.load(rnon, cinfo_s[0:1, 2:3])
                    pnv = gpsimd.snap(rpn, min_val=0, max_val=1)
                    gpsimd.reg_mov(rcnt, 0)
                    gpsimd.reg_mov(rval, 0)

                    def cond():
                        a = gpsimd.snap(rval, min_val=-2**31, max_val=2**31 - 1)
                        bq = gpsimd.snap(rnon, min_val=-2**31, max_val=2**31 - 1)
                        return a != bq
                    with gpsimd.While(cond):
                        gpsimd.reg_add(rcnt, rcnt, 16)
                        gpsimd.dma_start(out=pollt[0:1, 0:1],
                                         in_=ap["flg"][pnv, 0:1]
                                         ).then_inc(S["PRT"], 16)
                        gpsimd.wait_ge(S["PRT"], rcnt)
                        gpsimd.load(rval, pollt[0:1, 0:1])
                    gpsimd.sem_inc(S["PGO"], 1)
                emit_casts(len(cast_jobs))
                for g in range(NGRP):
                    for sl in range(NSLOT):
                        je = g * NSLOT + sl

                        def wd_dma(hs):
                            js = sl * NHS + hs
                            slot = hs if hs <= 5 else hs - 6
                            gpsimd.wait_ge(S["CAST"], 16 * (3 * js + 3))
                            if hs <= 5:
                                if je >= 1:
                                    gpsimd.wait_ge(S["DNM"], je * 2 * NDCF)
                            else:
                                gpsimd.wait_ge(S["DNM"], (je * 2 + 1) * NDCF)
                            gpsimd.dma_start(out=wde[:, slot, :], in_=wdsc[js]
                                             ).then_inc(S["WLD"], 16)

                        for hs in range(NHS):
                            wd_dma(hs)

        @block.tensor
        def _(tensor):
            if do_m:
                tensor.wait_ge(S["SET"], 3)
                tensor.matmul(P1[:, 0:512], ones1_s[:], opd_row[0:1, 0:512],
                              start=True, stop=True)
                tensor.matmul(P1[:, 512:1024], ones1_s[:], opd_row[0:1, 512:1024],
                              start=True, stop=True).then_inc(S["SET"], 1)
                tensor.wait_ge(S["LD"], NLD)
                tensor.wait_ge(S["LDG"], 32)
                for k in range(NCH):
                    for ts_ in range(4):
                        it = 4 * k + ts_
                        tensor.wait_ge(S["M0X"], 16 * (k + 1))
                        if it >= 2:
                            tensor.wait_ge(S["M0E"], it - 1)
                        sv = PB[3 - (ts_ % 2)][:].bitcast(BF16)
                        for dc in range(NDCF):
                            ins = tensor.transpose(
                                sv[:, dc * 128:(dc + 1) * 128],
                                m0_xb[:, ts_, dc * 128:(dc + 1) * 128],
                                identb_s[:])
                        ins.then_inc(S["M0T"], 1)
                    tensor.wait_ge(S["M0E"], 4 * (k + 1))
                    if k == 0:
                        tensor.wait_ge(S["SET"], 5)
                    else:
                        tensor.wait_ge(S["M0V"], k)
                    for c in range(NDCF):
                        tensor.matmul(P0[0:80, 0:512], wxT_s[:, c, 0:80],
                                      m0_xf[:, c, :], start=(c == 0),
                                      stop=(c == NDCF - 1))
                    for c in range(NDCF):
                        ins = tensor.matmul(P1[0:16, 0:512], wxT_s[:, c, 80:96],
                                            m0_xf[:, c, :], start=(c == 0),
                                            stop=(c == NDCF - 1))
                    ins.then_inc(S["M0M"], 1)
                for k in range(NCH):
                    for ts_ in range(4):
                        it = kt(k, ts_)
                        tensor.wait_ge(S["XTB"], 16 * (k + 1))
                        if it >= 2:
                            tensor.wait_ge(S["UEV"], it - 1)
                        elif k == 0:
                            tensor.wait_ge(S["M0E"], 4 * NCH)
                        half = (ts_ % 2) * 1024
                        sv = P3[:].bitcast(BF16)
                        for dc in range(NDC):
                            ins = tensor.transpose(
                                sv[:, half + dc * 128:half + (dc + 1) * 128],
                                x_tmb[:, ts_, dc * 128:(dc + 1) * 128],
                                identb_s[:])
                        ins.then_inc(S["UTR"], 1)
                    def bc_build(h, pieces=range(8)):
                        kh = k * 2 + h
                        for p in pieces:
                            if k == 0 and h == 0 and p == 0:
                                tensor.wait_ge(S["M0V"], NCH)
                            if 8 * kh + p >= 2:
                                tensor.wait_ge(S["BCE"], 8 * kh + p - 1)
                            if k >= 1 and h == 0 and p <= 1:
                                tensor.wait_ge(S["ZST"], 4 * k)
                            src = bm_s if p < 4 else cm_s
                            pp = p % 4
                            tgt = P1 if p % 2 == 0 else P2
                            for h2 in range(2):
                                ng = h * NH + 2 * pp + h2
                                ins = tensor.matmul(
                                    tgt[:, h2 * 512:(h2 + 1) * 512],
                                    seln_s[:, ng * 128:(ng + 1) * 128],
                                    src[:, k * TC:(k + 1) * TC],
                                    start=True, stop=True)
                            ins.then_inc(S["BCM"], 1)

                    def delta_mm(dc):
                        i = kd(k, dc)
                        if i == 0:
                            tensor.wait_ge(S["M0V"], NCH)
                        if i >= 2:
                            tensor.wait_ge(S["SPS"], i - 1)
                        tensor.matmul(
                            P0[:, (dc % 2) * 512:(dc % 2 + 1) * 512],
                            wdT_s[:, dc * 128:(dc + 1) * 128],
                            draw_s[:, k * TC:(k + 1) * TC],
                            start=True, stop=True).then_inc(S["DMM"], 1)

                    bc_build(0, range(0, 4))
                    delta_mm(0)
                    delta_mm(1)
                    bc_build(0, range(4, 8))
                    for dc in range(2, NDC):
                        delta_mm(dc)
                    bc_build(1)
                    tensor.wait_ge(S["YDC"], 2 * NDC * (k + 1))
                    tensor.wait_ge(S["BCE"], 8 * (2 * k + 2))
                    for ts_ in range(4):
                        it = kt(k, ts_)
                        if it >= 2:
                            tensor.wait_ge(S["ZST"], it - 1)
                        tgt = P1 if ts_ % 2 == 0 else P2
                        for dc in range(NDC):
                            ins = tensor.transpose(
                                tgt[:, dc * 128:(dc + 1) * 128],
                                yac[:, dc, ts_ * 128:(ts_ + 1) * 128],
                                ident_s[:])
                        ins.then_inc(S["ZTR"], 1)
            if do_f:
                if fused:
                    tensor.wait_ge(S["PGO"], 1)
                tensor.wait_ge(S["LD"], NLD)
                for g in range(NGRP):
                    gof = g * NTIL
                    if g >= 1:
                        tensor.wait_ge(S["DAC"], g * NSLOT * 2 * NDCF)
                        tensor.wait_ge(S["OEV"], 16 * g)
                    for i in range(NTIL):
                        gi = gof + i
                        tensor.wait_ge(S["LNN"], gi + 1)
                        pa, pb_ = (P0, P1) if i % 2 == 0 else (P2, P3)
                        if gi >= 2:
                            tensor.wait_ge(S["X2E"], gi - 1)
                            tensor.wait_ge(S["X2F"], gi - 1)
                        for dcf in range(NDCF):
                            tgt = pa if dcf < 8 else pb_
                            ins = tensor.transpose(
                                tgt[:, (dcf % 8) * 128:(dcf % 8 + 1) * 128],
                                x2t[:, 0, dcf * 128:(dcf + 1) * 128],
                                ident_s[:])
                        ins.then_inc(S["X2T"], 1)
                    tensor.wait_ge(S["X2E"], gof + NTIL)
                    tensor.wait_ge(S["X2F"], gof + NTIL)
                    if not FFN_GATE:
                        gate_skip = True
                    for tt in (range(2) if FFN_GATE else []):
                        for c in range(NDCF):
                            ins = tensor.matmul(
                                P2[0:8, tt * 512:(tt + 1) * 512], gwT_s[:, c, :],
                                acc[:, c, tt * 512:(tt + 1) * 512],
                                start=(c == 0), stop=(c == NDCF - 1))
                    if FFN_GATE:
                        ins.then_inc(S["GLM"], 1)
                    for tt in (range(2) if FFN_GATE else []):
                        for c in range(NDCF):
                            ins = tensor.matmul(
                                P0[0:1, tt * 512:(tt + 1) * 512], sgw_s[:, c:c + 1],
                                acc[:, c, tt * 512:(tt + 1) * 512],
                                start=(c == 0), stop=(c == NDCF - 1))
                    ins.then_inc(S["SGM"], 1)
                    tensor.wait_ge(S["GLE"], 2 * g + 1)
                    for tt in range(NTIL):
                        ins = tensor.transpose(P3[:, tt * 8:(tt + 1) * 8],
                                               lg_fm[0:8, tt * 128:(tt + 1) * 128],
                                               ident_s[0:8, 0:8])
                    ins.then_inc(S["GTT"], 1)
                    tensor.wait_ge(S["CMB"], g + 1)
                    for tt in range(NTIL):
                        ins = tensor.transpose(P2[0:8, tt * 128:(tt + 1) * 128],
                                               carg[:, tt, :], ident_s[:])
                    ins.then_inc(S["GTT"], 1)
                    tensor.wait_ge(S["SGE"], 2 * g + 1)
                    tensor.wait_ge(S["GLE"], 2 * g + 2)
                    for h2 in range(2):
                        ins = tensor.matmul(
                            P3[:, h2 * 512:(h2 + 1) * 512], ones1b_s[:],
                            sg_row[0:1, h2 * 512:(h2 + 1) * 512],
                            start=True, stop=True)
                    ins.then_inc(S["SGM"], 1)
                    for sl in range(NSLOT):
                        je = g * NSLOT + sl
                        if sl < E:
                            tensor.wait_ge(S["CFME"], g + 1)
                            if je >= 1:
                                tensor.wait_ge(S["SIL"], je * NHS)
                            else:
                                tensor.wait_ge(S["SGE"], 2 * g + 1)
                            for h2 in range(2):
                                ins = tensor.matmul(
                                    P0[:, h2 * 512:(h2 + 1) * 512],
                                    sel8_s[:, sl * 128:(sl + 1) * 128],
                                    comb_fm[0:8, h2 * 512:(h2 + 1) * 512],
                                    start=True, stop=True)
                            ins.then_inc(S["CBC"], 1)
                        for hs in range(NHS):
                            j = je * NHS + hs
                            tensor.wait_ge(S["WLG"], 16 * (j + 1))
                            if j >= 1:
                                tensor.wait_ge(S["SIL"], j)
                            if hs == 0 and sl < E:
                                tensor.wait_ge(S["CBE"], g * E + sl + 1)
                            if hs == 0 and sl == 0:
                                tensor.wait_ge(S["SGE"], 2 * g + 2)
                            for tt in range(2):
                                for c in range(NDCF):
                                    ins = tensor.matmul(
                                        P0[:, tt * 512:(tt + 1) * 512],
                                        wgb[:, 0, c * 128:(c + 1) * 128],
                                        x2b[:, c, tt * 512:(tt + 1) * 512],
                                        start=(c == 0), stop=(c == NDCF - 1))
                            ins.then_inc(S["GUM"], 1)
                            tensor.wait_ge(S["WLU"], 16 * (j + 1))
                            if j >= 1:
                                tensor.wait_ge(S["HT1"], j)
                            for tt in range(2):
                                for c in range(NDCF):
                                    ins = tensor.matmul(
                                        P1[:, tt * 512:(tt + 1) * 512],
                                        wub[:, 0, c * 128:(c + 1) * 128],
                                        x2b[:, c, tt * 512:(tt + 1) * 512],
                                        start=(c == 0), stop=(c == NDCF - 1))
                            ins.then_inc(S["UUM"], 1)
                        tensor.wait_ge(S["WLD"], 16 * (je * NHS + 6))
                        tensor.wait_ge(S["HPD"], je * NHS + 6)
                        for pas in range(2):
                            if pas == 1:
                                tensor.wait_ge(S["WLD"], 16 * (je + 1) * NHS)
                                tensor.wait_ge(S["HPD"], (je + 1) * NHS)
                            hs_list = ([0, 1, 2, 3, 4, 5] if pas == 0
                                       else [6, 7, 8, 9, 10])
                            if sl == 0 and pas == 0:
                                tensor.wait_ge(S["CFME"], g + 1)
                                tensor.wait_ge(S["SGE"], 2 * (g + 1))
                            for dcf in range(NDCF):
                                jd = (je * 2 + pas) * NDCF + dcf
                                if jd >= 2:
                                    tensor.wait_ge(S["DAC"], jd - 1)
                                tgt = P2 if dcf % 2 == 0 else P3
                                for tt in range(2):
                                    for ii, hs in enumerate(hs_list):
                                        slot = hs if hs <= 5 else hs - 6
                                        ins = tensor.matmul(
                                            tgt[:, tt * 512:(tt + 1) * 512],
                                            wde[:, slot,
                                                dcf * 128:(dcf + 1) * 128],
                                            hpb[:, hs, tt * 512:(tt + 1) * 512],
                                            start=(ii == 0),
                                            stop=(ii == len(hs_list) - 1))
                                ins.then_inc(S["DNM"], 1)
                    dacb = ((g + 1) * NSLOT * 2 - 1) * NDCF
                    for i in range(NTIL):
                        gi = gof + i
                        if gi >= 1:
                            tensor.wait_ge(S["OEV"], 2 * gi)
                        for dcf in range(NDCF):
                            if i == 0:
                                tensor.wait_ge(S["DAC"], dacb + dcf + 1)
                            tgt = P0 if dcf < 8 else P1
                            ins = tensor.transpose(
                                tgt[:, (dcf % 8) * 128:(dcf % 8 + 1) * 128],
                                acc[:, dcf, i * 128:(i + 1) * 128], ident_s[:])
                        ins.then_inc(S["OTR"], 1)

        @block.vector
        def _(vector):
            mix = [0]

            def vop(ins):
                mix[0] += 1
                ins.then_inc(S["MIX"], 1)

            def vwait():
                vector.wait_ge(S["MIX"], mix[0])

            vector.wait_ge(S["LD"], NLD)
            parc = [0]

            def pwait(ins, bit):
                if PARANOID & (1 << bit):
                    parc[0] += 1
                    ins.then_inc(S["PAR"], 1)
                    vector.wait_ge(S["PAR"], parc[0])
                return ins
            if do_m:
                vector.wait_ge(S["SET"], 1)
                vector.tensor_scalar(out=acol_s[:], in0=acol_s[:], scalar1=-1.0,
                                     scalar2=None, op0=OP.mult).then_inc(S["SET"], 1)
                vector.tensor_scalar(out=opd_row[:], in0=opd_row[:], scalar1=1.0,
                                     scalar2=None, op0=OP.add).then_inc(S["SET"], 1)
                for k in range(NCH):
                    for h in range(2):
                        kh = 2 * k + h
                        for dc in range(NDC):
                            i = kd(k, dc)
                            ih = kh * NDC + dc
                            vector.wait_ge(S["DBX"], ih + 1)
                            vector.wait_ge(S["DAS"], ih + 1)
                            for n in range(NH):
                                ng = h * NH + n
                                init = 0.0 if k == 0 else \
                                    state_s[:, dc, ng:ng + 1]
                                ins = pwait(vector.tensor_tensor_scan(
                                    out=hsc[:, n, :], data0=dA[:, n, :],
                                    data1=dbx[:, dc % 2, n * TC:(n + 1) * TC],
                                    initial=init,
                                    op0=OP.mult, op1=OP.add), 3)
                            ins.then_inc(S["SCN"], 1)
                            fine = (h == 0 and dc == 0)
                            vector.wait_ge(S["BCE"],
                                           8 * kh + (5 if fine else 8))
                            if k >= 1 and dc == 0 and h == 0:
                                vector.wait_ge(S["ZTR"], 4 * k)
                            if h == 0:
                                vector.tensor_mul(out=yac[:, dc, :],
                                                  in0=hsc[:, 0, :],
                                                  in1=crep[:, 0, :])
                                planes = list(range(1, NH))
                            else:
                                planes = list(range(NH))

                            def ym(j):
                                n = planes[j]
                                if fine and n % 2 == 0:
                                    vector.wait_ge(S["BCE"],
                                                   8 * kh + 5 + n // 2)
                                vector.tensor_mul(out=ytmp[:, j % 4, :],
                                                  in0=hsc[:, n, :],
                                                  in1=crep[:, n, :])
                            ym(0)
                            ym(1)
                            for j in range(len(planes)):
                                if j + 2 < len(planes):
                                    ym(j + 2)
                                if j == len(planes) - 1:
                                    vector.tensor_copy(
                                        out=state_s[:, dc,
                                                    h * NH:(h + 1) * NH],
                                        in_=hsc[:, :, TC - 1])
                                parc[0] += 1
                                vector.tensor_add(out=yac[:, dc, :],
                                                  in0=yac[:, dc, :],
                                                  in1=ytmp[:, j % 4, :]
                                                  ).then_inc(S["PAR"], 1)
                                vector.wait_ge(S["PAR"], parc[0])
                            vector.sem_inc(S["YDC"], 1)
                    for ts_ in range(4):
                        it = kt(k, ts_)
                        vector.wait_ge(S["ZTR"], it + 1)
                        vector.wait_ge(S["WTM"], it + 1)
                        src = P1 if ts_ % 2 == 0 else P2
                        vector.scalar_tensor_tensor(
                            out=x_tm[:, ts_, :], in0=src[:], scalar=1.0,
                            in1=wtm[:, ts_, :], op0=OP.mult,
                            op1=OP.add).then_inc(S["ZST"], 1)
            if do_f:
                if fused:
                    vector.wait_ge(S["PGO"], 1)
                inv_d = 1.0 / D

                def ln_micro():
                    vop(vector.tensor_scalar(out=stat[:, 2:3], in0=stat[:, 0:1],
                                             scalar1=inv_d, scalar2=None,
                                             op0=OP.mult))
                    vop(vector.tensor_scalar(out=stat[:, 3:4], in0=stat[:, 1:2],
                                             scalar1=inv_d, scalar2=None,
                                             op0=OP.mult))
                    vwait()
                    vop(vector.tensor_scalar(out=stat[:, 4:5], in0=stat[:, 2:3],
                                             scalar1=stat[:, 2:3], scalar2=None,
                                             op0=OP.mult))
                    vwait()
                    vop(vector.tensor_sub(out=stat[:, 4:5], in0=stat[:, 3:4],
                                          in1=stat[:, 4:5]))
                    vwait()
                    vop(vector.tensor_scalar(out=stat[:, 4:5], in0=stat[:, 4:5],
                                             scalar1=EPS, scalar2=None,
                                             op0=OP.add))
                    vwait()
                    sqrt_wait.append(mix[0])
                    vector.wait_ge(S["MICS"], len(sqrt_wait))
                    vop(vector.reciprocal(out=stat[:, 6:7], in_=stat[:, 5:6]))
                    vwait()

                for g in range(NGRP):
                    gof = g * NTIL
                    for i in range(NTIL):
                        gi = gof + i
                        vector.wait_ge(S["ZLD"], 16 * (gi + 1))
                        vop(vector.reduce_sum(out=stat[:, 0:1],
                                              in_=ztile[:, gi % 2, :],
                                              axis=AX.X))
                        vector.wait_ge(S["LNS"], gi + 1)
                        vwait()
                        ln_micro()
                        if gi >= 1:
                            vector.wait_ge(S["X2T"], gi)
                        vector.tensor_scalar(out=x2t[:, 0, :],
                                             in0=ztile[:, gi % 2, :],
                                             scalar1=stat[:, 2:3],
                                             scalar2=stat[:, 6:7],
                                             op0=OP.subtract,
                                             op1=OP.mult).then_inc(S["LNN"], 1)
                        vector.wait_ge(S["X2T"], gi + 1)
                        vector.wait_ge(S["X2E"], gi + 1)
                        pa, pb_ = (P0, P1) if i % 2 == 0 else (P2, P3)
                        vector.tensor_copy(out=acc[:, 0:8, i * 128:(i + 1) * 128],
                                           in_=pa[:].rearrange(
                                               "p (c h) -> p c h", c=8))
                        vector.tensor_copy(out=acc[:, 8:16, i * 128:(i + 1) * 128],
                                           in_=pb_[:].rearrange(
                                               "p (c h) -> p c h", c=8)
                                           ).then_inc(S["X2F"], 1)
                    vector.wait_ge(S["GLE"], 2 * g + 2)
                    for tt in range(NTIL):
                        lgv = lgT[:, tt, :]
                        vop(vector.reduce_max(out=stat[:, 8:9], in_=lgv,
                                              axis=AX.X))
                        vwait()
                        vop(vector.tensor_scalar(out=cmask[:, tt, :], in0=lgv,
                                                 scalar1=stat[:, 8:9],
                                                 scalar2=-1e30,
                                                 op0=OP.is_ge, op1=OP.mult))
                        vwait()
                        vop(vector.tensor_add(out=carg[:, tt, :], in0=lgv,
                                              in1=cmask[:, tt, :]))
                        vwait()
                        vop(vector.reduce_max(out=stat[:, 9:10],
                                              in_=carg[:, tt, :], axis=AX.X))
                        vwait()
                        vop(vector.tensor_scalar(out=cmask[:, tt, :], in0=lgv,
                                                 scalar1=stat[:, 9:10],
                                                 scalar2=None, op0=OP.is_ge))
                        vop(vector.tensor_tensor(out=stat[:, 10:11],
                                                 in0=stat[:, 8:9],
                                                 in1=stat[:, 9:10], op=OP.add))
                        vwait()
                        vop(vector.tensor_scalar(out=carg[:, tt, :], in0=lgv,
                                                 scalar1=2.0,
                                                 scalar2=stat[:, 10:11],
                                                 op0=OP.mult, op1=OP.subtract))
                        vwait()
                    vector.sem_inc(S["TOP"], 1)
                    vector.wait_ge(S["SIG"], g + 1)
                    vector.tensor_mul(out=carg[:], in0=carg[:],
                                      in1=cmask[:]).then_inc(S["CMB"], 1)
                    if g >= 1:
                        vector.wait_ge(S["DNM"], g * NSLOT * 2 * NDCF)
                    for sl in range(NSLOT):
                        je = g * NSLOT + sl
                        if sl < E:
                            vector.wait_ge(S["CBE"], g * E + sl + 1)
                        else:
                            vector.wait_ge(S["SGE"], 2 * g + 2)
                        for hs in range(NHS):
                            j = je * NHS + hs
                            vector.wait_ge(S["SIL"], j + 1)
                            vector.wait_ge(S["UUM"], j + 1)
                            vector.tensor_mul(out=hpt[:], in0=outt[:, 0:GTOK],
                                              in1=P1[:]).then_inc(S["HT1"], 1)
                            vector.wait_ge(S["HT1"], j + 1)
                            scl = sg_rep[:] if sl >= E else comb_e[:, sl % 2, :]
                            vector.tensor_mul(out=hpb[:, hs, :], in0=hpt[:],
                                              in1=scl).then_inc(S["HPD"], 1)
                            vector.wait_ge(S["HPD"], j + 1)
                        for pas in range(2):
                            for dcf in range(NDCF):
                                jd = (je * 2 + pas) * NDCF + dcf
                                vector.wait_ge(S["DNM"], jd + 1)
                                src = P2 if dcf % 2 == 0 else P3
                                vector.tensor_add(out=acc[:, dcf, :],
                                                  in0=acc[:, dcf, :],
                                                  in1=src[:]).then_inc(S["DAC"], 1)
                    vector.wait_ge(S["ZLD"], 16 * NTIL * (g + 1))
                    for i in range(NTIL):
                        gi = gof + i
                        vector.wait_ge(S["LN2S"], gi + 1)
                        vop(vector.reduce_sum(out=stat[:, 0:1], in_=outt[:],
                                              axis=AX.X))
                        vwait()
                        ln_micro()
                        if gi >= 2:
                            vector.wait_ge(S["ODM"], 16 * (gi - 1))
                        vector.tensor_scalar(out=ztile[:, gi % 2, :], in0=outt[:],
                                             scalar1=stat[:, 2:3],
                                             scalar2=stat[:, 6:7],
                                             op0=OP.subtract,
                                             op1=OP.mult).then_inc(S["LN2N"], 1)

        @block.scalar
        def _(scalar):
            scalar.wait_ge(S["LD"], NLD)
            if do_m:
                scalar.activation(out=acol_s[:], in_=acol_s[:],
                                  func=AF.Exp).then_inc(S["SET"], 1)
                scalar.wait_ge(S["SET"], 4)
                scalar.copy(out=opd_rep[:], in_=P1[:]).then_inc(S["SET"], 1)
                for k in range(NCH):
                    for ts_ in range(4):
                        it = 4 * k + ts_
                        scalar.wait_ge(S["M0T"], it + 1)
                        sv = PB[3 - (ts_ % 2)][:].bitcast(BF16)
                        scalar.copy(
                            out=m0_xf[:, :, ts_ * 128:(ts_ + 1) * 128],
                            in_=sv.rearrange("p (c h) -> p c h", c=NDCF),
                        ).then_inc(S["M0E"], 1)
                    scalar.wait_ge(S["M0M"], k + 1)
                    scalar.copy(out=draw_s[:, k * TC:(k + 1) * TC],
                                in_=P0[0:64, 0:512])
                    scalar.copy(out=bm_s[:, k * TC:(k + 1) * TC],
                                in_=P0[64:80, 0:512])
                    scalar.copy(out=cm_s[:, k * TC:(k + 1) * TC],
                                in_=P1[0:16, 0:512]).then_inc(S["M0V"], 1)
                for k in range(NCH):
                    for ts_ in range(4):
                        it = kt(k, ts_)
                        scalar.wait_ge(S["UTR"], it + 1)
                        half = (ts_ % 2) * 1024
                        scalar.copy(
                            out=xfmh[:, :, ts_ * 128:(ts_ + 1) * 128],
                            in_=P3[:].bitcast(BF16)[:, half:half + 1024]
                            .rearrange("p (c h) -> p c h", c=NDC),
                        ).then_inc(S["UEV"], 1)

                    def bc_evacs(h, pieces=range(8)):
                        kh = 2 * k + h
                        for p in pieces:
                            scalar.wait_ge(S["BCM"], 8 * kh + p + 1)
                            if kh >= 1:
                                scalar.wait_ge(S["SCN"] if p < 4 else S["YDC"],
                                               NDC * kh)
                            src = P1 if p % 2 == 0 else P2
                            dstt = brep if p < 4 else crep
                            pp = p % 4
                            scalar.copy(
                                out=dstt[:, 2 * pp:2 * pp + 2, :],
                                in_=src[:].rearrange("p (a t) -> p a t", a=2),
                            ).then_inc(S["BCE"], 1)

                    def exp_dc(h, dc):
                        i = kd(k, dc)
                        ih = (2 * k + h) * NDC + dc
                        if h == 0:
                            scalar.wait_ge(S["SPP"], i + 1)
                        if ih >= 1:
                            scalar.wait_ge(S["SCN"], ih)
                        for n in range(NH):
                            ng = h * NH + n
                            ins = scalar.activation(
                                out=dA[:, n, :], in_=delta_s[:, dc, :],
                                func=AF.Exp,
                                scale=acol_s[:, dc * N + ng:dc * N + ng + 1])
                        ins.then_inc(S["DAS"], 1)

                    bc_evacs(0, range(0, 4))
                    for dc in range(NDC):
                        i = kd(k, dc)
                        scalar.wait_ge(S["DMM"], i + 1)
                        if i >= 2:
                            scalar.wait_ge(S["SPP"], i - 1)
                        ps = P0[:, (dc % 2) * 512:(dc % 2 + 1) * 512]
                        bb = bdt_s[:, dc:dc + 1]
                        scalar.activation(out=sp1[:, i % 2, :], in_=ps,
                                          func=AF.Abs, bias=bb)
                        scalar.activation(out=sp2[:, i % 2, :], in_=ps,
                                          func=AF.Relu,
                                          bias=bb).then_inc(S["SPS"], 1)
                        scalar.activation(out=sp1[:, i % 2, :],
                                          in_=sp1[:, i % 2, :],
                                          func=AF.Exp, scale=-1.0)
                        scalar.activation(out=sp1[:, i % 2, :],
                                          in_=sp1[:, i % 2, :],
                                          func=AF.Ln,
                                          bias=1.0).then_inc(S["SPA"], 1)
                        exp_dc(0, dc)
                        if dc == 0:
                            bc_evacs(0, [4, 5])
                        elif dc == 1:
                            bc_evacs(0, [6, 7])
                    bc_evacs(1)
                    for dc in range(NDC):
                        exp_dc(1, dc)
            if do_f:
                if fused:
                    scalar.wait_ge(S["PGO"], 1)
                    rzp = scalar.alloc_register("rzp")
                    scalar.load(rzp, cinfo_s[0:1, 0:1])
                    zdhv = scalar.snap(rzp, min_val=0, max_val=1)
                    zsh_ld = ap["zsh"].rearrange(
                        "(u v w p) d -> v u w p d", p=128, w=4, v=2)[zdhv]
                q = 0

                def zload(gi2):
                    g2, i2 = divmod(gi2, NTIL)
                    if gi2 >= 2:
                        scalar.wait_ge(S["LNN"], gi2 - 1)
                    if g2 >= 1 and i2 < 2:
                        scalar.wait_ge(S["ODM"], 16 * (NTIL * g2 - 1 + i2))
                    if mode == "ffn":
                        scalar.dma_start(
                            out=ztile[:, gi2 % 2, :],
                            in_=ap["zin"][gi2 * 128:(gi2 + 1) * 128, :],
                        ).then_inc(S["ZLD"], 16)
                    else:
                        scalar.dma_start(
                            out=ztile[:, gi2 % 2, :],
                            in_=zsh_ld[2 * g2 + i2 // 4, i2 % 4],
                        ).then_inc(S["ZLD"], 16)

                for g in range(NGRP):
                    gof = g * NTIL
                    if g >= 1:
                        scalar.wait_ge(S["GUM"], g * NSLOT * NHS)
                        scalar.wait_ge(S["UUM"], g * NSLOT * NHS)
                    zload(gof)
                    zload(gof + 1)
                    for i in range(NTIL):
                        gi = gof + i
                        if g >= 1 and i == 0:
                            scalar.wait_ge(S["LN2N"], NTIL * g)
                        scalar.wait_ge(S["ZLD"], 16 * (gi + 1))
                        scalar.activation(out=outt[:], in_=ztile[:, gi % 2, :],
                                          func=AF.Square,
                                          accum_out=stat[:, 1:2]).then_inc(S["LNS"], 1)
                        scalar.wait_ge(S["MIX"], sqrt_wait[q])
                        scalar.activation(out=stat[:, 5:6], in_=stat[:, 4:5],
                                          func=AF.Sqrt).then_inc(S["MICS"], 1)
                        q += 1
                        if i + 2 < NTIL:
                            zload(gi + 2)
                        scalar.wait_ge(S["X2T"], gi + 1)
                        pa, pb_ = (P0, P1) if i % 2 == 0 else (P2, P3)
                        scalar.copy(out=x2b[:, 0:8, i * 128:(i + 1) * 128],
                                    in_=pa[:].rearrange("p (c h) -> p c h", c=8))
                        scalar.copy(out=x2b[:, 8:16, i * 128:(i + 1) * 128],
                                    in_=pb_[:].rearrange("p (c h) -> p c h", c=8)
                                    ).then_inc(S["X2E"], 1)
                    scalar.wait_ge(S["GLM"], g + 1)
                    scalar.copy(out=lg_fm[:], in_=P2[0:8, :]).then_inc(S["GLE"], 1)
                    scalar.wait_ge(S["SGM"], 2 * g + 1)
                    scalar.activation(out=sg_row[:], in_=P0[0:1, :],
                                      func=AF.Sigmoid).then_inc(S["SGE"], 1)
                    scalar.wait_ge(S["GTT"], 2 * g + 1)
                    scalar.copy(out=lgT[:],
                                in_=P3[:, 0:NTIL * 8]
                                .rearrange("p (t e) -> p t e", t=NTIL)
                                ).then_inc(S["GLE"], 1)
                    scalar.wait_ge(S["TOP"], g + 1)
                    scalar.activation(out=carg[:], in_=carg[:],
                                      func=AF.Sigmoid).then_inc(S["SIG"], 1)
                    scalar.wait_ge(S["GTT"], 2 * g + 2)
                    scalar.copy(out=comb_fm[:], in_=P2[0:8, :]).then_inc(S["CFME"], 1)
                    scalar.wait_ge(S["SGM"], 2 * g + 2)
                    if g >= 1:
                        scalar.wait_ge(S["HPD"], g * NSLOT * NHS)
                    scalar.copy(out=sg_rep[:], in_=P3[:]).then_inc(S["SGE"], 1)
                    for sl in range(NSLOT):
                        je = g * NSLOT + sl
                        if sl < E:
                            scalar.wait_ge(S["CBC"], g * E + sl + 1)
                            if g * E + sl >= 2:
                                scalar.wait_ge(S["HPD"], (je - 1) * NHS)
                            scalar.copy(out=comb_e[:, sl % 2, :],
                                        in_=P0[:]).then_inc(S["CBE"], 1)
                        for hs in range(NHS):
                            j = je * NHS + hs
                            scalar.wait_ge(S["GUM"], j + 1)
                            if j >= 1:
                                scalar.wait_ge(S["HT1"], j)
                            scalar.activation(out=outt[:, 0:GTOK], in_=P0[:],
                                              func=AF.Silu).then_inc(S["SIL"], 1)
                    for i in range(NTIL):
                        gi = gof + i
                        scalar.wait_ge(S["OTR"], gi + 1)
                        if i == 0:
                            scalar.wait_ge(S["X2T"], NTIL * (g + 1))
                        if gi >= 1:
                            scalar.wait_ge(S["LN2N"], gi)
                        scalar.copy(out=outt[:, 0:1024], in_=P0[:]).then_inc(S["OEV"], 1)
                        scalar.copy(out=outt[:, 1024:2048],
                                    in_=P1[:]).then_inc(S["OEV"], 1)
                        scalar.activation(out=x2t[:, 0, :], in_=outt[:],
                                          func=AF.Square,
                                          accum_out=stat[:, 1:2]).then_inc(S["LN2S"], 1)
                        scalar.wait_ge(S["MIX"], sqrt_wait[q])
                        scalar.activation(out=stat[:, 5:6], in_=stat[:, 4:5],
                                          func=AF.Sqrt).then_inc(S["MICS"], 1)
                        q += 1

    return nc



def _sel_matrix(nrows, nsel, width=128):
    m = np.zeros((nrows, nsel * width), ml_dtypes.bfloat16)
    for n in range(nsel):
        if n < nrows:
            m[n, n * width:(n + 1) * width] = 1.0
    return m


def _sel_n():
    m = np.zeros((16, N * 128), ml_dtypes.bfloat16)
    for ng in range(N):
        m[ng, ng * 128:(ng + 1) * 128] = 1.0
    return m


def _sel_8():
    m = np.zeros((8, E * 128), ml_dtypes.bfloat16)
    for e in range(E):
        m[e, e * 128:(e + 1) * 128] = 1.0
    return m


def _prep_core_inputs(c, full, nonce, mode):
    b_, dh = c // 2, c % 2
    d0 = dh * DH
    f32 = np.float32
    out = {
        "ident": np.eye(128, dtype=f32),
        "identb": np.eye(128, dtype=ml_dtypes.bfloat16),
        "ones1": np.ones((1, 128), f32),
        "ones1b": np.ones((1, 128), ml_dtypes.bfloat16),
        "cinfo": np.array([[dh, 1 - dh, nonce, 0, 0, 0, 0, 0]], np.int32),
    }
    if mode in ("mamba", "fused"):
        xb = np.ascontiguousarray(np.asarray(full["L0_attn_x"][b_], f32))
        out["x"] = xb
        out["xh"] = np.ascontiguousarray(xb[:, d0:d0 + DH])
        out["wxT"] = np.ascontiguousarray(np.asarray(full["L0_attn_W_xproj"], f32).T)
        out["wdT"] = np.ascontiguousarray(
            np.asarray(full["L0_attn_W_dt"], f32)[d0:d0 + DH].T)
        out["bdt"] = np.ascontiguousarray(
            np.asarray(full["L0_attn_b_dt"], f32)[d0:d0 + DH].reshape(NDC, 128).T)
        out["alog"] = np.ascontiguousarray(
            np.asarray(full["L0_attn_A_log"], f32)[d0:d0 + DH]
            .reshape(NDC, 128, N).transpose(1, 0, 2).reshape(128, NDC * N))
        out["dprm"] = np.ascontiguousarray(
            np.asarray(full["L0_attn_D_param"], f32).reshape(1, D)[:, d0:d0 + DH])
        out["seln"] = _sel_n()
    if mode in ("ffn", "fused"):
        out["gwT"] = np.ascontiguousarray(np.asarray(full["L0_ffn_gate_w"], f32).T)
        out["sgw"] = np.ascontiguousarray(
            np.asarray(full["L0_ffn_shgate_w"], f32).reshape(NDCF, 128).T)
        out["sel8"] = _sel_8()
        out["wg"] = np.ascontiguousarray(np.asarray(full["L0_ffn_w_gate"], f32))
        out["wu"] = np.ascontiguousarray(np.asarray(full["L0_ffn_w_up"], f32))
        out["wd"] = np.ascontiguousarray(np.asarray(full["L0_ffn_w_down"], f32))
        out["swg"] = np.ascontiguousarray(np.asarray(full["L0_ffn_shw_gate"], f32))
        out["swu"] = np.ascontiguousarray(np.asarray(full["L0_ffn_shw_up"], f32))
        out["swd"] = np.ascontiguousarray(np.asarray(full["L0_ffn_shw_down"], f32))
    if mode == "ffn":
        pass
    return out


def _own_rows(dh):
    return np.concatenate([np.arange(ch * TC, (ch + 1) * TC)
                           for ch in (dh, dh + 2, dh + 4, dh + 6)])


def _scatter_out(res, key):
    out = np.empty((B, L, D), np.float32)
    for c in range(8):
        b_, dh = c // 2, c % 2
        oy = np.asarray(res.results[c][key]).reshape(4, TC, D)
        for j, ch in enumerate((dh, dh + 2, dh + 4, dh + 6)):
            out[b_, ch * TC:(ch + 1) * TC] = oy[j]
    return out


KERNEL_MODE = "fused"


def kernel(**inputs):
    if KERNEL_MODE == "fused":
        nonce = int(np.random.default_rng().integers(1, 2**31 - 1))
        nc = build_program("fused")
        in_maps = [_prep_core_inputs(c, inputs, nonce, "fused") for c in range(8)]
        res = run_bass_kernel_spmd(nc, in_maps, core_ids=list(range(8)))
        return _scatter_out(res, "oy")
    nc_m = build_program("mamba")
    in_maps = [_prep_core_inputs(c, inputs, 0, "mamba") for c in range(8)]
    res_m = run_bass_kernel_spmd(nc_m, in_maps, core_ids=list(range(8)))
    z = np.empty((B, L, D), np.float32)
    for c in range(8):
        b_, dh = c // 2, c % 2
        z[b_, :, dh * DH:(dh + 1) * DH] = np.asarray(res_m.results[c]["zout"])
    nc_f = build_program("ffn")
    in_maps = []
    for c in range(8):
        b_, dh = c // 2, c % 2
        m = _prep_core_inputs(c, inputs, 0, "ffn")
        m["zin"] = np.ascontiguousarray(z[b_][_own_rows(dh)])
        in_maps.append(m)
    res_f = run_bass_kernel_spmd(nc_f, in_maps, core_ids=list(range(8)))
    return _scatter_out(res_f, "oy")
